# version 04 (final)
# baseline (speedup 1.0000x reference)
import numpy as np
import concourse.bass as bass
import concourse.mybir as mybir
from concourse.bass import broadcast_tensor_aps
from concourse.bass_utils import run_bass_kernel_spmd

F32 = mybir.dt.float32
BF16 = mybir.dt.bfloat16
AF = mybir.ActivationFunctionType
ALU = mybir.AluOpType
AX = mybir.AxisListType

S = 2048
D = 1024
NT = S // 128
NC = D // 128


def host_constants():
    c = {}
    c["ident"] = np.eye(128, dtype=np.float32)
    c["ones128"] = np.ones((128, 128), dtype=np.float32)

    hd = 64
    half = np.arange(0, hd, 2, dtype=np.float64)
    inv_freq = 1.0 / 10000.0 ** (half / hd)
    t = np.arange(S, dtype=np.float64)
    freqs = t[:, None] * inv_freq[None, :]
    freqs = np.concatenate([freqs, freqs], -1)
    base_scale = (half + 0.4 * hd) / (1.4 * hd)
    power = (t - S // 2) / (256 // 2)
    scale = base_scale[None, :] ** power[:, None]
    scale = np.concatenate([scale, scale], -1)
    cos = np.cos(freqs); sin = np.sin(freqs)
    sigma = np.where(np.arange(hd) < 32, -1.0, 1.0)[None, :]
    c["rot_csq"] = (cos * scale).astype(np.float32)
    c["rot_ssq"] = (sin * scale * sigma).astype(np.float32)
    c["rot_csk"] = (cos / scale).astype(np.float32)
    c["rot_ssk"] = (sin / scale * sigma).astype(np.float32)

    NEG = -1.0e30
    i_rel = np.arange(128)[:, None]
    j_rel = np.arange(384)[None, :]
    allowed = (j_rel >= i_rel) & (j_rel <= i_rel + 256)
    c["mask_gen"] = np.where(allowed, 0.0, NEG).astype(np.float32)
    j0 = np.arange(128)[None, :]
    c["mask_t0"] = np.where(j0 <= i_rel, 0.0, NEG).astype(np.float32)
    j1 = np.arange(256)[None, :]
    c["mask_t1"] = np.where(j1 <= 128 + i_rel, 0.0, NEG).astype(np.float32)

    jj = np.arange(128)[:, None]; ii = np.arange(128)[None, :]
    c["triT128"] = (jj <= ii).astype(np.float32)
    jj16 = np.arange(16)[:, None]; ii16 = np.arange(16)[None, :]
    tri16 = (jj16 <= ii16).astype(np.float32)
    c["triT16x8"] = np.tile(tri16, (1, 8)).astype(np.float32)
    c["bsel128"] = np.concatenate([np.zeros(32, np.float32),
                                   np.full(64, 1.0 / 64.0, np.float32)])
    c["constv"] = np.asarray(CONSTS, dtype=np.float32)
    return c



class Sem:
    __slots__ = ("h", "n")
    def __init__(self, nc, name):
        self.h = nc.alloc_semaphore(name=name)
        self.n = 0
    def wait(self, eng, val=None):
        eng.wait_ge(self.h, self.n if val is None else val)


CONSTS = [1e-5, 1e-6, -0.5, 1.0 / D, -1.0 / D, 1.0, 0.0, 8.0, 75.0, -75.0,
          1e-24, 2.0, -1.0, 0.5, 1.0 / 64.0, 1e-12]


class Kd:
    def __init__(self, nc):
        self.nc = nc
        self._bar = Sem(nc, "kbar")
        self._sems = {}
        self._constp = None

    def sem(self, name):
        if name not in self._sems:
            self._sems[name] = Sem(self.nc, name)
        return self._sems[name]

    def init_consts(self, constv_hbm):
        nc = self.nc
        n = len(CONSTS)
        self._constp = nc.alloc_sbuf_tensor("kconstp", [128, n], F32).ap()
        s = self.sem("kconst_ld")
        cp = self._constp
        with nc.Block() as B:
            @B.sync
            def _(e):
                src, dst = broadcast_tensor_aps(
                    constv_hbm.rearrange("(p n) -> p n", p=1), cp)
                e.dma_start(out=dst, in_=src).then_inc(s.h, 16)
                s.n += 16
        self.barrier_dma(s)

    def barrier_dma(self, sem):
        nc = self.nc
        with nc.Block() as B:
            for setter in (B.sync, B.gpsimd, B.vector, B.scalar, B.tensor):
                def mk(st):
                    @st
                    def _(e):
                        sem.wait(e)
                mk(setter)

    def cf(self, val):
        idx = CONSTS.index(val)
        return self._constp[:, idx:idx + 1]

    def barrier(self):
        nc = self.nc
        bar = self._bar
        target = bar.n + 5
        with nc.Block() as B:
            for setter in (B.sync, B.gpsimd, B.vector, B.scalar, B.tensor):
                def mk(st):
                    @st
                    def _(e):
                        e.drain()
                        e.nop().then_inc(bar.h, 1)
                        e.wait_ge(bar.h, target)
                mk(setter)
        bar.n = target


def milestone(e, sem, k=1):
    e.drain()
    e.nop().then_inc(sem.h, k)
    sem.n += k



def stage_prep(K, x_in, ident, xF, xnF):
    nc = K.nc
    s_ld = K.sem("p_ld")
    s_v = K.sem("p_v")
    s_a = K.sem("p_a")
    s_t = K.sem("p_t")
    s_st = [K.sem("p_st0"), K.sem("p_st1")]
    with (
        nc.sbuf_tensor("p_xT", [128, NT * D], F32) as xT_,
        nc.sbuf_tensor("p_xnT", [128, NT * D], F32) as xnT_,
        nc.sbuf_tensor("p_idt", [128, 128], F32) as idt_,
        nc.sbuf_tensor("p_stat", [128, NT * 8], F32) as stat_,
        nc.sbuf_tensor("p_sq", [128, D], F32) as sq_,
        nc.sbuf_tensor("p_ev", [128, 2 * NC * 2 * 128], F32) as ev_,
        nc.psum_tensor("p_psA", [128, 128], F32) as psA_,
        nc.psum_tensor("p_psB", [128, 128], F32) as psB_,
        nc.psum_tensor("p_ps2A", [128, 128], F32) as ps2A_,
        nc.psum_tensor("p_ps2B", [128, 128], F32) as ps2B_,
    ):
        xT, xnT, idt = xT_.ap(), xnT_.ap(), idt_.ap()
        stat, sq, ev = stat_.ap(), sq_.ap(), ev_.ap()
        ps = [psA_.ap(), psB_.ap()]
        ps2 = [ps2A_.ap(), ps2B_.ap()]

        with nc.Block() as B:
            @B.sync
            def _(e):
                e.dma_start(out=xT.rearrange("p (t d) -> p t d", d=D),
                            in_=x_in.rearrange("(t p) d -> p t d", p=128)).then_inc(s_ld.h, 16)
                s_ld.n += 16
                e.dma_start(out=idt, in_=ident).then_inc(s_ld.h, 16)
                s_ld.n += 16

            @B.vector
            def _(e):
                s_ld.wait(e)
                for tt in range(NT):
                    xtile = xT[:, tt * D:(tt + 1) * D]
                    st = stat[:, 8 * tt: 8 * tt + 8]
                    e.tensor_mul(sq, xtile, xtile)
                    e.reduce_sum(st[:, 0:1], xtile, AX.X)
                    e.drain()
                    e.reduce_sum(st[:, 1:2], sq, AX.X)
                    e.drain()
                    e.tensor_scalar_mul(st[:, 2:3], st[:, 0:1], -1.0 / D)
                    e.drain()
                    e.tensor_mul(st[:, 3:4], st[:, 2:3], st[:, 2:3])
                    e.drain()
                    e.scalar_tensor_tensor(st[:, 4:5], st[:, 1:2], 1.0 / D, st[:, 3:4],
                                           ALU.mult, ALU.subtract)
                milestone(e, s_v)

            @B.scalar
            def _(e):
                s_v.wait(e)
                for tt in range(NT):
                    st = stat[:, 8 * tt: 8 * tt + 8]
                    e.activation(st[:, 5:6], st[:, 4:5], AF.Ln, bias=K.cf(1e-5))
                    e.drain()
                    e.activation(st[:, 6:7], st[:, 5:6], AF.Exp, scale=K.cf(-0.5))
                    e.drain()
                milestone(e, s_a)

            @B.vector
            def _(e):
                s_a.wait(e)
                for tt in range(NT):
                    xtile = xT[:, tt * D:(tt + 1) * D]
                    st = stat[:, 8 * tt: 8 * tt + 8]
                    e.tensor_scalar(xnT[:, tt * D:(tt + 1) * D], xtile,
                                    st[:, 2:3], st[:, 6:7], ALU.add, ALU.mult)
                milestone(e, s_v)

        t0, v0 = s_t.n, s_v.n
        st0 = [s_st[0].n, s_st[1].n]
        with nc.Block() as B:
            @B.tensor
            def _(e):
                s_v.wait(e)
                for i in range(NT * NC):
                    tt, cc = divmod(i, NC)
                    if i >= 2:
                        e.wait_ge(s_v.h, v0 + (i - 1))
                    off = tt * D + cc * 128
                    e.transpose(ps[i % 2], xT[:, off:off + 128], idt)
                    e.transpose(ps2[i % 2], xnT[:, off:off + 128], idt)
                    e.drain()
                    e.nop().then_inc(s_t.h, 1)
                s_t.n += NT * NC

            @B.vector
            def _(e):
                for i in range(NT * NC):
                    tt, cc = divmod(i, NC)
                    e.wait_ge(s_t.h, t0 + i + 1)
                    if tt >= 2 and cc == 0:
                        e.wait_ge(s_st[tt % 2].h, st0[tt % 2] + 32 * (tt // 2))
                    slot = (tt % 2) * NC * 256
                    e.tensor_copy(ev[:, slot + cc * 256: slot + cc * 256 + 128],
                                  ps[i % 2])
                    e.tensor_copy(ev[:, slot + cc * 256 + 128: slot + cc * 256 + 256],
                                  ps2[i % 2])
                    e.drain()
                    e.nop().then_inc(s_v.h, 1)
                s_v.n += NT * NC

            @B.sync
            def _(e):
                for tt in range(NT):
                    e.wait_ge(s_v.h, v0 + (tt + 1) * NC)
                    slot = (tt % 2) * NC * 256
                    evv = ev[:, slot: slot + NC * 256].rearrange("p (c u d) -> p c u d", c=NC, u=2)
                    ss = s_st[tt % 2]
                    e.dma_start(out=xF[:, :, tt * 128:(tt + 1) * 128].rearrange("c p d -> p c d"),
                                in_=evv[:, :, 0, :]).then_inc(ss.h, 16)
                    e.dma_start(out=xnF[:, :, tt * 128:(tt + 1) * 128].rearrange("c p d -> p c d"),
                                in_=evv[:, :, 1, :]).then_inc(ss.h, 16)
                    ss.n += 32
                s_st[0].wait(e)
                s_st[1].wait(e)
    K.barrier()



def stage_attn0(K, xnF, xF, Wq, bq, Wk, bk, Wv, bv, Wo, bo, ident, triT128, accF):
    nc = K.nc
    CH = 16
    H = 16
    VW = 1032

    s_gld = K.sem("a_gld")
    s_ld = K.sem("a_ld")
    s_t = K.sem("a_t")
    s_v = K.sem("a_v")
    s_a = K.sem("a_a")
    s_g = K.sem("a_g")
    s_sp = [K.sem("a_sp0"), K.sem("a_sp1")]
    s_wl = [K.sem("a_wl0"), K.sem("a_wl1")]
    s_st = [K.sem("a_st0"), K.sem("a_st1")]

    o0spill = nc.dram_tensor("a_o0spill", [NT, 128, D], BF16, kind="Internal").ap()

    esO = ExitStack()
    with esO:
        alO = lambda n, sh, dt: esO.enter_context(nc.sbuf_tensor(n, sh, dt)).ap()
        idt = alO("a_idt", [128, 128], F32)
        idtb = alO("a_idtb", [128, 128], BF16)
        tri = alO("a_tri", [128, 128], F32)
        biasT = alO("a_bias", [128, 4 * NC], F32)
        saugE = alO("a_saugE", [128, 8 * 65], BF16)
        saugO = alO("a_saugO", [128, 8 * 65], BF16)

        with (
            nc.sbuf_tensor("a_brow", [8, 4 * 128], F32) as brow_,
            nc.psum_tensor("a_psb", [128, 4 * 128], F32) as psb_,
        ):
            brow, psb = brow_.ap(), psb_.ap()
            with nc.Block() as B:
                @B.sync
                def _(e):
                    e.dma_start(out=idt, in_=ident).then_inc(s_ld.h, 16); s_ld.n += 16
                    e.dma_start(out=tri, in_=triT128).then_inc(s_ld.h, 16); s_ld.n += 16
                    for i, b in enumerate((bq, bk, bv, bo)):
                        e.dma_start(out=brow[:, i * 128:(i + 1) * 128],
                                    in_=b.rearrange("(a p) -> a p", a=8)).then_inc(s_ld.h, 16)
                        s_ld.n += 16
                @B.vector
                def _(e):
                    s_ld.wait(e)
                    e.tensor_copy(idtb, idt)
                    e.memset(saugE, 0.0)
                    e.memset(saugO, 0.0)
                    milestone(e, s_v)
                @B.tensor
                def _(e):
                    s_ld.wait(e)
                    for i in range(4):
                        e.transpose(psb[:, i * 128: i * 128 + 8],
                                    brow[:, i * 128:(i + 1) * 128], idt[0:8, 0:8])
                    e.drain()
                    e.nop().then_inc(s_t.h, 1); s_t.n += 1
                @B.vector
                def _(e):
                    s_t.wait(e)
                    for i in range(4):
                        for cc in range(NC):
                            e.tensor_copy(biasT[:, 4 * cc + i: 4 * cc + i + 1],
                                          psb[:, i * 128 + cc: i * 128 + cc + 1])
                    milestone(e, s_v)
        K.barrier()

        esQ = ExitStack()
        with esQ:
            alQ = lambda n, sh, dt: esQ.enter_context(nc.sbuf_tensor(n, sh, dt)).ap()
            qF = alQ("a_qF", [128, NC * S], BF16)
            kF = alQ("a_kF", [128, NC * S], BF16)
            u_ap = alQ("a_u", [128, NT * D], BF16)
            vT = alQ("a_vT", [128, NT * VW], BF16)
            vF = u_ap
            kT = u_ap

            esP = ExitStack()
            with esP:
                alP = lambda n, sh, dt: esP.enter_context(nc.sbuf_tensor(n, sh, dt)).ap()
                palP = lambda n, sh, dt: esP.enter_context(nc.psum_tensor(n, sh, dt)).ap()
                xnb = alP("a_xnb", [128, NC * 1024], BF16)
                wT = alP("a_wT", [128, NC * D], BF16)
                wnat = alP("a_wnat", [128, 2 * D], BF16)
                ph1 = alP("a_ph1", [128, 2 * 512], F32)
                ph2 = alP("a_ph2", [128, 2 * 512], F32)
                pw = [palP("a_pwA", [128, 128], BF16), palP("a_pwB", [128, 128], BF16)]
                pp = [palP("a_pp0", [128, 512], F32), palP("a_pp1", [128, 512], F32)]

                with nc.Block() as B:
                    @B.vector
                    def _(e):
                        e.memset(vT.rearrange("p (t w) -> p t w", w=VW)[:, :, 1024:VW], 1.0)
                        milestone(e, s_v)
                K.barrier()

                for wi, (Wmat, outk) in enumerate(((Wq, "q"), (Wk, "k"), (Wv, "v"))):
                    t0 = s_t.n; v0 = s_v.n
                    wl0 = [s_wl[0].n, s_wl[1].n]
                    with nc.Block() as B:
                        @B.gpsimd
                        def _(e):
                            for a in range(NC):
                                if a >= 2:
                                    e.wait_ge(s_t.h, t0 + (a - 1) * NC)
                                e.dma_start(out=wnat[:, (a % 2) * D:(a % 2) * D + D],
                                            in_=Wmat[a * 128:(a + 1) * 128, :]).then_inc(s_wl[a % 2].h, 16)
                                s_wl[a % 2].n += 16
                        @B.tensor
                        def _(e):
                            for a in range(NC):
                                e.wait_ge(s_wl[a % 2].h, wl0[a % 2] + 16 * (a // 2 + 1))
                                for cc in range(NC):
                                    i = a * NC + cc
                                    if i >= 2:
                                        e.wait_ge(s_v.h, v0 + i - 1)
                                    e.transpose(pw[i % 2],
                                                wnat[:, (a % 2) * D + cc * 128:(a % 2) * D + (cc + 1) * 128],
                                                idtb)
                                    e.drain()
                                    e.nop().then_inc(s_t.h, 1)
                            s_t.n += NC * NC
                        @B.vector
                        def _(e):
                            for a in range(NC):
                                for cc in range(NC):
                                    i = a * NC + cc
                                    e.wait_ge(s_t.h, t0 + i + 1)
                                    e.tensor_copy(wT[:, cc * D + a * 128: cc * D + a * 128 + 128],
                                                  pw[i % 2])
                                    e.drain()
                                    e.nop().then_inc(s_v.h, 1)
                            s_v.n += NC * NC
                    K.barrier()

                    for half in range(2):
                        toff = half * 1024
                        with nc.Block() as B:
                            @B.gpsimd
                            def _(e):
                                e.dma_start(out=xnb.rearrange("p (c t) -> p c t", c=NC),
                                            in_=xnF[:, :, toff: toff + 1024].rearrange("c p t -> p c t")).then_inc(s_gld.h, 16)
                                s_gld.n += 16
                        NTile = NC * 2
                        t0 = s_t.n; v0 = s_v.n; a0 = s_a.n; gg0 = s_g.n
                        with nc.Block() as B:
                            @B.tensor
                            def _(e):
                                s_gld.wait(e)
                                for i in range(NTile):
                                    cc, ts = divmod(i, 2)
                                    if i >= 2:
                                        e.wait_ge(s_v.h, v0 + (i - 1))
                                        if outk in ("q", "k"):
                                            e.wait_ge(s_a.h, a0 + (i - 1))
                                    for dd in range(NC):
                                        e.matmul(pp[i % 2],
                                                 wT[:, dd * D + cc * 128: dd * D + cc * 128 + 128],
                                                 xnb[:, dd * 1024 + ts * 512: dd * 1024 + (ts + 1) * 512],
                                                 start=(dd == 0), stop=(dd == NC - 1))
                                    e.drain()
                                    e.nop().then_inc(s_t.h, 1)
                                s_t.n += NTile
                            if outk in ("q", "k"):
                                dst = qF if outk == "q" else kF
                                bofs = 0 if outk == "q" else 1
                                @B.scalar
                                def _(e):
                                    for i in range(NTile):
                                        cc, ts = divmod(i, 2)
                                        e.wait_ge(s_t.h, t0 + i + 1)
                                        if i >= 2:
                                            e.wait_ge(s_v.h, v0 + i - 1)
                                        e.activation(ph2[:, (i % 2) * 512:(i % 2) * 512 + 512],
                                                     pp[i % 2], AF.Exp,
                                                     bias=biasT[:, 4 * cc + bofs: 4 * cc + bofs + 1])
                                        e.activation(ph1[:, (i % 2) * 512:(i % 2) * 512 + 512],
                                                     pp[i % 2], AF.Relu,
                                                     bias=biasT[:, 4 * cc + bofs: 4 * cc + bofs + 1])
                                        e.drain()
                                        e.nop().then_inc(s_a.h, 1)
                                    s_a.n += NTile
                                @B.vector
                                def _(e):
                                    for i in range(NTile):
                                        cc, ts = divmod(i, 2)
                                        e.wait_ge(s_a.h, a0 + i + 1)
                                        e.scalar_tensor_tensor(
                                            dst[:, cc * S + toff + ts * 512:
                                                cc * S + toff + (ts + 1) * 512],
                                            ph2[:, (i % 2) * 512:(i % 2) * 512 + 512], 1.0,
                                            ph1[:, (i % 2) * 512:(i % 2) * 512 + 512],
                                            ALU.min, ALU.add)
                                        e.drain()
                                        e.nop().then_inc(s_v.h, 1)
                                    s_v.n += NTile
                            else:
                                @B.vector
                                def _(e):
                                    for i in range(NTile):
                                        cc, ts = divmod(i, 2)
                                        e.wait_ge(s_t.h, t0 + i + 1)
                                        e.tensor_scalar(vF[:, cc * S + toff + ts * 512:
                                                           cc * S + toff + (ts + 1) * 512],
                                                        pp[i % 2],
                                                        biasT[:, 4 * cc + 2: 4 * cc + 3],
                                                        None, ALU.add)
                                        e.drain()
                                        e.nop().then_inc(s_v.h, 1)
                                    s_v.n += NTile
                        K.barrier()

                for (srcF, kind) in ((vF, "v"), (kF, "k")):
                    t0 = s_t.n; v0 = s_v.n
                    with nc.Block() as B:
                        @B.tensor
                        def _(e):
                            for i in range(NT * NC):
                                tt, cc = divmod(i, NC)
                                if i >= 2:
                                    e.wait_ge(s_v.h, v0 + i - 1)
                                e.transpose(pw[i % 2],
                                            srcF[:, cc * S + tt * 128: cc * S + tt * 128 + 128],
                                            idtb)
                                e.drain()
                                e.nop().then_inc(s_t.h, 1)
                            s_t.n += NT * NC
                        @B.vector
                        def _(e):
                            for i in range(NT * NC):
                                tt, cc = divmod(i, NC)
                                e.wait_ge(s_t.h, t0 + i + 1)
                                if kind == "v":
                                    dstap = vT[:, tt * VW + cc * 128: tt * VW + cc * 128 + 128]
                                else:
                                    dstap = kT[:, tt * D + cc * 128: tt * D + cc * 128 + 128]
                                e.tensor_copy(dstap, pw[i % 2])
                                e.drain()
                                e.nop().then_inc(s_v.h, 1)
                            s_v.n += NT * NC
                    K.barrier()

            esS = ExitStack()
            with esS:
                alS = lambda n, sh, dt: esS.enter_context(nc.sbuf_tensor(n, sh, dt)).ap()
                palS = lambda n, sh, dt: esS.enter_context(nc.psum_tensor(n, sh, dt)).ap()
                o0T = alS("a_o0T", [128, 2 * D], BF16)
                pmt = alS("a_pmt", [128, 2 * 128], BF16)
                sc = alS("a_sc", [128, 16], F32)
                pP = [palS("a_pPA", [128, 128], F32), palS("a_pPB", [128, 128], F32)]
                pN = [palS("a_pN0", [128, 512], F32), palS("a_pN1", [128, 512], F32)]
                pU = [palS("a_pU0", [128, 512], F32), palS("a_pU1", [128, 512], F32)]

                s_pt = K.sem("a_pt")
                s_pm = K.sem("a_pm")
                s_nt = K.sem("a_nt")
                s_nv = K.sem("a_nv")
                s_su = K.sem("a_su")
                s_sa = K.sem("a_sa")

                def num_evac(e, gp):
                    cp, hp = divmod(gp, H)
                    e.wait_ge(s_nt.h, gp + 1)
                    if cp >= 2 and hp == 0:
                        e.wait_ge(s_sp[cp % 2].h, 16 * (cp // 2))
                    e.tensor_scalar(sc[:, 0:1], pN[gp % 2][:, 64:65], 1e-6, None, ALU.add)
                    e.drain()
                    e.reciprocal(sc[:, 1:2], sc[:, 0:1])
                    e.drain()
                    e.tensor_scalar(o0T[:, (cp % 2) * D + hp * 64: (cp % 2) * D + hp * 64 + 64],
                                    pN[gp % 2][:, 0:64], sc[:, 1:2], None, ALU.mult)
                    e.drain()
                    e.nop().then_inc(s_nv.h, 1); s_nv.n += 1

                def saug_add(e, gp):
                    cp, hp = divmod(gp, H)
                    hc, par = divmod(hp, 2)
                    e.wait_ge(s_su.h, gp + 1)
                    e.wait_ge(s_nt.h, cp * H + 2 * hc + 2)
                    if par == 0:
                        dst = saugE[0:64, hc * 65:(hc + 1) * 65]
                        srcp = pU[gp % 2][0:64, 0:65]
                    else:
                        dst = saugO[64:128, hc * 65:(hc + 1) * 65]
                        srcp = pU[gp % 2][64:128, 0:65]
                    e.tensor_add(dst, dst, srcp)
                    e.drain()
                    e.nop().then_inc(s_sa.h, 1); s_sa.n += 1

                with nc.Block() as B:
                    @B.vector
                    def _(e):
                        for c in range(CH):
                            for h in range(H):
                                gi = c * H + h
                                e.wait_ge(s_pt.h, gi + 1)
                                if gi >= 2:
                                    e.wait_ge(s_nt.h, gi - 1)
                                e.tensor_mul(pmt[:, (gi % 2) * 128:(gi % 2) * 128 + 128],
                                             pP[gi % 2], tri)
                                e.drain()
                                e.nop().then_inc(s_pm.h, 1); s_pm.n += 1
                                if gi >= 1:
                                    num_evac(e, gi - 1)
                                    saug_add(e, gi - 1)
                        num_evac(e, CH * H - 1)
                        saug_add(e, CH * H - 1)

                    @B.tensor
                    def _(e):
                        for c in range(CH):
                            for h in range(H):
                                gi = c * H + h
                                hc, par = divmod(h, 2)
                                if gi >= 2:
                                    e.wait_ge(s_pm.h, gi - 1)
                                e.matmul(pP[gi % 2],
                                         kF[64 * par:64 * par + 64,
                                            hc * S + c * 128: hc * S + c * 128 + 128],
                                         qF[64 * par:64 * par + 64,
                                            hc * S + c * 128: hc * S + c * 128 + 128],
                                         start=True, stop=True)
                                e.drain()
                                e.nop().then_inc(s_pt.h, 1)
                                s_pt.n += 1
                                e.wait_ge(s_pm.h, gi + 1)
                                if gi >= 2:
                                    e.wait_ge(s_nv.h, gi - 1)
                                if c > 0 and h == 0:
                                    e.wait_ge(s_sa.h, c * H)
                                sa = (saugE if par == 0 else saugO)[:, hc * 65:(hc + 1) * 65]
                                e.matmul(pN[gi % 2][:, 0:64],
                                         pmt[:, (gi % 2) * 128:(gi % 2) * 128 + 128],
                                         vT[:, c * VW + h * 64: c * VW + h * 64 + 64],
                                         start=True, stop=False)
                                e.matmul(pN[gi % 2][:, 64:65],
                                         pmt[:, (gi % 2) * 128:(gi % 2) * 128 + 128],
                                         vT[:, c * VW + 1024: c * VW + 1025],
                                         start=False, stop=False)
                                e.matmul(pN[gi % 2][:, 0:65],
                                         qF[:, hc * S + c * 128: hc * S + c * 128 + 128],
                                         sa, start=False, stop=True)
                                e.drain()
                                e.nop().then_inc(s_nt.h, 1); s_nt.n += 1
                                if gi >= 2:
                                    e.wait_ge(s_sa.h, gi - 1)
                                tp = (0, 64 * par)
                                rows = pU[gi % 2][64 * par:64 * par + 64, 0:65]
                                e.matmul(rows[:, 0:64],
                                         kT[:, c * D + h * 64: c * D + h * 64 + 64],
                                         vT[:, c * VW + h * 64: c * VW + h * 64 + 64],
                                         start=True, stop=False, tile_position=tp)
                                e.matmul(rows[:, 64:65],
                                         kT[:, c * D + h * 64: c * D + h * 64 + 64],
                                         vT[:, c * VW + 1024: c * VW + 1025],
                                         start=False, stop=True, tile_position=tp)
                                e.drain()
                                e.nop().then_inc(s_su.h, 1); s_su.n += 1

                    @B.sync
                    def _(e):
                        for c in range(CH):
                            e.wait_ge(s_nv.h, (c + 1) * H)
                            e.dma_start(out=o0spill[c],
                                        in_=o0T[:, (c % 2) * D:(c % 2) * D + D]
                                        ).then_inc(s_sp[c % 2].h, 16)
                            s_sp[c % 2].n += 16
                        s_sp[0].wait(e)
                        s_sp[1].wait(e)
            K.barrier()

        esF = ExitStack()
        with esF:
            alF = lambda n, sh, dt: esF.enter_context(nc.sbuf_tensor(n, sh, dt)).ap()
            palF = lambda n, sh, dt: esF.enter_context(nc.psum_tensor(n, sh, dt)).ap()
            o0F = alF("a_o0F", [128, NC * S], BF16)
            woT = alF("a_woT", [128, NC * D], BF16)
            wnat2 = alF("a_wnat2", [128, 2 * D], BF16)
            ldT = alF("a_ldT", [128, 2 * D], BF16)
            xfl = alF("a_xfl", [128, 2 * 512], F32)
            acc = alF("a_acc", [128, 2 * 512], F32)
            pw2 = [palF("a_pw2A", [128, 128], BF16), palF("a_pw2B", [128, 128], BF16)]
            po = [palF("a_po0", [128, 512], F32), palF("a_po1", [128, 512], F32)]

            t0 = s_t.n; v0 = s_v.n
            wl0 = [s_wl[0].n, s_wl[1].n]
            with nc.Block() as B:
                @B.gpsimd
                def _(e):
                    for a in range(NC):
                        if a >= 2:
                            e.wait_ge(s_t.h, t0 + (a - 1) * NC)
                        e.dma_start(out=wnat2[:, (a % 2) * D:(a % 2) * D + D],
                                    in_=Wo[a * 128:(a + 1) * 128, :]).then_inc(s_wl[a % 2].h, 16)
                        s_wl[a % 2].n += 16
                @B.tensor
                def _(e):
                    for a in range(NC):
                        e.wait_ge(s_wl[a % 2].h, wl0[a % 2] + 16 * (a // 2 + 1))
                        for cc in range(NC):
                            i = a * NC + cc
                            if i >= 2:
                                e.wait_ge(s_v.h, v0 + i - 1)
                            e.transpose(pw2[i % 2],
                                        wnat2[:, (a % 2) * D + cc * 128:(a % 2) * D + (cc + 1) * 128],
                                        idtb)
                            e.drain()
                            e.nop().then_inc(s_t.h, 1)
                    s_t.n += NC * NC
                @B.vector
                def _(e):
                    for a in range(NC):
                        for cc in range(NC):
                            i = a * NC + cc
                            e.wait_ge(s_t.h, t0 + i + 1)
                            e.tensor_copy(woT[:, cc * D + a * 128: cc * D + a * 128 + 128],
                                          pw2[i % 2])
                            e.drain()
                            e.nop().then_inc(s_v.h, 1)
                    s_v.n += NC * NC
            K.barrier()

            t0 = s_t.n; v0 = s_v.n
            wl0 = [s_wl[0].n, s_wl[1].n]
            with nc.Block() as B:
                @B.gpsimd
                def _(e):
                    for tt in range(NT):
                        if tt >= 2:
                            e.wait_ge(s_t.h, t0 + (tt - 1) * NC)
                        e.dma_start(out=ldT[:, (tt % 2) * D:(tt % 2) * D + D],
                                    in_=o0spill[tt]).then_inc(s_wl[tt % 2].h, 16)
                        s_wl[tt % 2].n += 16
                @B.tensor
                def _(e):
                    for tt in range(NT):
                        e.wait_ge(s_wl[tt % 2].h, wl0[tt % 2] + 16 * (tt // 2 + 1))
                        for cc in range(NC):
                            i = tt * NC + cc
                            if i >= 2:
                                e.wait_ge(s_v.h, v0 + i - 1)
                            e.transpose(pw2[i % 2],
                                        ldT[:, (tt % 2) * D + cc * 128:(tt % 2) * D + (cc + 1) * 128],
                                        idtb)
                            e.drain()
                            e.nop().then_inc(s_t.h, 1)
                    s_t.n += NT * NC
                @B.vector
                def _(e):
                    for tt in range(NT):
                        for cc in range(NC):
                            i = tt * NC + cc
                            e.wait_ge(s_t.h, t0 + i + 1)
                            e.tensor_copy(o0F[:, cc * S + tt * 128: cc * S + tt * 128 + 128],
                                          pw2[i % 2])
                            e.drain()
                            e.nop().then_inc(s_v.h, 1)
                    s_v.n += NT * NC
            K.barrier()

            NTile = NC * 4
            t0 = s_t.n; v0 = s_v.n
            wl0 = [s_wl[0].n, s_wl[1].n]
            with nc.Block() as B:
                @B.gpsimd
                def _(e):
                    for i in range(NTile):
                        cc, ts = divmod(i, 4)
                        if i >= 2:
                            e.wait_ge(s_v.h, v0 + i - 1)
                        e.dma_start(out=xfl[:, (i % 2) * 512:(i % 2) * 512 + 512],
                                    in_=xF[cc, :, ts * 512:(ts + 1) * 512]).then_inc(s_wl[i % 2].h, 16)
                        s_wl[i % 2].n += 16
                @B.tensor
                def _(e):
                    for i in range(NTile):
                        cc, ts = divmod(i, 4)
                        if i >= 2:
                            e.wait_ge(s_v.h, v0 + i - 1)
                        for dd in range(NC):
                            e.matmul(po[i % 2],
                                     woT[:, dd * D + cc * 128: dd * D + cc * 128 + 128],
                                     o0F[:, dd * S + ts * 512: dd * S + (ts + 1) * 512],
                                     start=(dd == 0), stop=(dd == NC - 1))
                        e.drain()
                        e.nop().then_inc(s_t.h, 1)
                    s_t.n += NTile
                @B.vector
                def _(e):
                    for i in range(NTile):
                        cc, ts = divmod(i, 4)
                        e.wait_ge(s_t.h, t0 + i + 1)
                        e.wait_ge(s_wl[i % 2].h, wl0[i % 2] + 16 * (i // 2 + 1))
                        if i >= 2:
                            e.wait_ge(s_st[i % 2].h, 16 * (i // 2))
                        e.scalar_tensor_tensor(acc[:, (i % 2) * 512:(i % 2) * 512 + 512],
                                               xfl[:, (i % 2) * 512:(i % 2) * 512 + 512],
                                               biasT[:, 4 * cc + 3: 4 * cc + 4],
                                               po[i % 2], ALU.add, ALU.add)
                        e.drain()
                        e.nop().then_inc(s_v.h, 1)
                    s_v.n += NTile
                @B.sync
                def _(e):
                    for i in range(NTile):
                        cc, ts = divmod(i, 4)
                        e.wait_ge(s_v.h, v0 + i + 1)
                        e.dma_start(out=accF[cc, :, ts * 512:(ts + 1) * 512],
                                    in_=acc[:, (i % 2) * 512:(i % 2) * 512 + 512]
                                    ).then_inc(s_st[i % 2].h, 16)
                        s_st[i % 2].n += 16
                    s_st[0].wait(e)
                    s_st[1].wait(e)
    K.barrier()



from contextlib import ExitStack


def stage_ffn_layer(K, li, yFi, yFo, Wqkv, q_scale, k_scale, Wo, ln_g, ln_b,
                    W1, b1, W2, b2, ident, ones128, csq, ssq, csk, ssk,
                    mask_gen, mask_t0, mask_t1):
    nc = K.nc
    H = 16
    DH = 64
    Lp = f"f{li}_"

    s_gld = K.sem(Lp + "gld")
    s_ld = K.sem(Lp + "ld")
    s_t = K.sem(Lp + "t")
    s_v = K.sem(Lp + "v")
    s_a = K.sem(Lp + "a")
    s_g = K.sem(Lp + "g")
    s_wl = [K.sem(Lp + "wl0"), K.sem(Lp + "wl1")]
    s_st = [K.sem(Lp + "st0"), K.sem(Lp + "st1")]
    s_yl = [K.sem(Lp + "yl0"), K.sem(Lp + "yl1")]

    es = ExitStack()
    alloc = lambda n, sh, dt: es.enter_context(nc.sbuf_tensor(Lp + n, sh, dt)).ap()
    palloc = lambda n, sh, dt: es.enter_context(nc.psum_tensor(Lp + n, sh, dt)).ap()

    yMid = nc.dram_tensor(Lp + "yMid", [NC, 128, S], F32, kind="Internal").ap()
    ysb = alloc("ysb", [128, NC * S], BF16)
    idt = alloc("idt", [128, 128], F32)
    idtb = alloc("idtb", [128, 128], BF16)
    ones = alloc("ones", [128, 128], F32)
    mskg = alloc("mskg", [128, 384], F32)
    msk0 = alloc("msk0", [128, 128], F32)
    msk1 = alloc("msk1", [128, 256], F32)
    scv = alloc("scv", [128, 4 * DH], F32)
    fvec = alloc("fvec", [128, 96], F32)

    with nc.Block() as B:
        @B.sync
        def _(e):
            e.dma_start(out=idt, in_=ident).then_inc(s_ld.h, 16); s_ld.n += 16
            e.dma_start(out=ones, in_=ones128).then_inc(s_ld.h, 16); s_ld.n += 16
            e.dma_start(out=mskg, in_=mask_gen).then_inc(s_ld.h, 16); s_ld.n += 16
            e.dma_start(out=msk0, in_=mask_t0).then_inc(s_ld.h, 16); s_ld.n += 16
            e.dma_start(out=msk1, in_=mask_t1).then_inc(s_ld.h, 16); s_ld.n += 16
        @B.gpsimd
        def _(e):
            e.dma_start(out=ysb[:, 0: 4 * S].rearrange("p (c t) -> p c t", c=4),
                        in_=yFi[0:4].rearrange("c p t -> p c t")).then_inc(s_gld.h, 16)
            s_gld.n += 16
            e.dma_start(out=ysb.rearrange("p (c t) -> p c t", c=NC)[:, 4:8, 1:S],
                        in_=yFi[4:8, :, 0:S - 1].rearrange("c p t -> p c t")).then_inc(s_gld.h, 16)
            s_gld.n += 16
        @B.vector
        def _(e):
            s_ld.wait(e)
            s_gld.wait(e)
            e.tensor_copy(idtb, idt)
            for cc in range(4, NC):
                e.memset(ysb[:, cc * S: cc * S + 1], 0.0)
            milestone(e, s_v)
    K.barrier()

    esA = ExitStack()
    allocA = lambda n, sh, dt: esA.enter_context(nc.sbuf_tensor(Lp + n, sh, dt)).ap()
    pallocA = lambda n, sh, dt: esA.enter_context(nc.psum_tensor(Lp + n, sh, dt)).ap()
    tabs = allocA("tabs", [128, 4 * NT * DH], BF16)
    TL = NT * DH
    with ExitStack() as es2:
        tabf = es2.enter_context(nc.sbuf_tensor(Lp + "tabf", [128, NT * DH], F32)).ap()
        qsrow = es2.enter_context(nc.sbuf_tensor(Lp + "qsrow", [1, 2 * DH], F32)).ap()
        pqs = es2.enter_context(nc.psum_tensor(Lp + "pqs", [128, 4 * DH], F32)).ap()
        with nc.Block() as B:
            @B.sync
            def _(e):
                e.dma_start(out=qsrow[:, 0:DH],
                            in_=q_scale[li].rearrange("(a d) -> a d", a=1)).then_inc(s_ld.h, 16)
                s_ld.n += 16
                e.dma_start(out=qsrow[:, DH:2 * DH],
                            in_=k_scale[li].rearrange("(a d) -> a d", a=1)).then_inc(s_ld.h, 16)
                s_ld.n += 16
            @B.tensor
            def _(e):
                s_ld.wait(e)
                for i, (off, sw) in enumerate(((0, 0), (0, 1), (DH, 0), (DH, 1))):
                    if sw:
                        srow = qsrow[:, off:off + DH].rearrange("a (h e) -> a h e", h=2)[:, ::-1, :]
                        dst = pqs[:, i * DH:(i + 1) * DH].rearrange("p (h e) -> p h e", h=2)
                    else:
                        srow = qsrow[:, off:off + DH]
                        dst = pqs[:, i * DH:(i + 1) * DH]
                    e.matmul(dst, ones[0:1, :], srow, start=True, stop=True)
                e.drain()
                e.nop().then_inc(s_t.h, 1); s_t.n += 1
            @B.vector
            def _(e):
                s_t.wait(e)
                e.tensor_copy(scv[:, 0:DH], pqs[:, 0:DH])
                e.tensor_copy(scv[:, 2 * DH:3 * DH], pqs[:, DH:2 * DH])
                e.tensor_copy(scv[:, DH:2 * DH], pqs[:, 2 * DH:3 * DH])
                e.tensor_copy(scv[:, 3 * DH:4 * DH], pqs[:, 3 * DH:4 * DH])
                milestone(e, s_v)
        for ti, tab in enumerate((csq, ssq, csk, ssk)):
            mul_off = {0: 0, 1: 2 * DH, 2: DH, 3: 3 * DH}[ti]
            with nc.Block() as B:
                @B.sync
                def _(e):
                    if ti > 0:
                        s_v.wait(e)
                    e.dma_start(out=tabf, in_=tab).then_inc(s_ld.h, 16); s_ld.n += 16
                @B.vector
                def _(e):
                    s_ld.wait(e)
                    a3 = tabf.rearrange("p (t e) -> p t e", e=DH)
                    m3 = scv[:, mul_off:mul_off + DH].rearrange("p (a e) -> p a e", a=1)
                    a3b, m3b = broadcast_tensor_aps(a3, m3)
                    e.tensor_tensor(tabs[:, ti * TL:(ti + 1) * TL].rearrange("p (t e) -> p t e", e=DH),
                                    a3b, m3b, ALU.mult)
                    milestone(e, s_v)
    K.barrier()

    wqkv = allocA("wqkv", [128, NC * 3072], BF16)
    qkT = allocA("qkT", [128, 2 * D], F32)
    sqs = allocA("sqs", [128, D], F32)
    rn = allocA("rn", [128, 64], F32)
    rot1 = allocA("rot1", [128, D], F32)
    rot2 = allocA("rot2", [128, D], F32)
    qrot = allocA("qrot", [128, D], BF16)
    krot = allocA("krot", [128, D], BF16)
    vstg = allocA("vstg", [128, D], BF16)
    qrotF = allocA("qrotF", [128, D], BF16)
    kring = allocA("kring", [128, 3 * D], BF16)
    vring = allocA("vring", [128, 3 * D], BF16)
    simsb = allocA("simsb", [128, 384], F32)
    psb = allocA("psb", [128, 384], BF16)
    p2sb = allocA("p2sb", [128, 384], BF16)
    pTs = allocA("pTs", [128, 384], BF16)
    aoF = allocA("aoF", [128, NC * 512], BF16)
    wos = allocA("wos", [128, 2 * D], BF16)
    ybuf = allocA("ybuf", [128, 8 * 512], F32)
    smx = allocA("smx", [128, 8], F32)

    pqkv = [pallocA("pqkv0", [128, 512], F32), pallocA("pqkv1", [128, 512], F32)]
    ptr = pallocA("ptr", [128, 128], BF16)
    psim = pallocA("psim", [128, 384], F32)
    ppt = pallocA("ppt", [128, 384], BF16)
    ppv = pallocA("ppv", [128, 128], F32)
    pwo = [pallocA("pwo0", [128, 512], F32), pallocA("pwo1", [128, 512], F32)]

    with nc.Block() as B:
        @B.gpsimd
        def _(e):
            e.dma_start(out=wqkv.rearrange("p (c n) -> p c n", c=NC),
                        in_=Wqkv[li].rearrange("(c p) n -> p c n", p=128)).then_inc(s_gld.h, 16)
            s_gld.n += 16
    K.barrier()

    for tt in range(NT):
        nj = min(tt + 1, 3)
        W = nj * 128
        msk = (msk0, msk1, mskg)[nj - 1]
        slots = [(tt - (nj - 1) + b) % 3 for b in range(nj)]
        t0 = s_t.n; v0 = s_v.n
        with nc.Block() as B:
            @B.tensor
            def _(e):
                if tt == 0:
                    s_gld.wait(e)
                for grp in range(6):
                    if grp >= 2:
                        e.wait_ge(s_v.h, v0 + grp - 1)
                    for dd in range(NC):
                        e.matmul(pqkv[grp % 2],
                                 ysb[:, dd * S + tt * 128: dd * S + (tt + 1) * 128],
                                 wqkv[:, dd * 3072 + grp * 512: dd * 3072 + (grp + 1) * 512],
                                 start=(dd == 0), stop=(dd == NC - 1))
                    e.drain()
                    e.nop().then_inc(s_t.h, 1)
                s_t.n += 6
            @B.vector
            def _(e):
                for grp in range(6):
                    e.wait_ge(s_t.h, t0 + grp + 1)
                    if grp < 2:
                        e.tensor_copy(qkT[:, grp * 512:(grp + 1) * 512], pqkv[grp % 2])
                    elif grp < 4:
                        e.tensor_copy(qkT[:, D + (grp - 2) * 512: D + (grp - 1) * 512],
                                      pqkv[grp % 2])
                    else:
                        e.tensor_copy(vstg[:, (grp - 4) * 512:(grp - 3) * 512], pqkv[grp % 2])
                    e.drain()
                    e.nop().then_inc(s_v.h, 1)
                s_v.n += 6
        with nc.Block() as B:
            @B.vector
            def _(e):
                for qi in range(2):
                    src = qkT[:, qi * D:(qi + 1) * D]
                    e.tensor_mul(sqs, src, src)
                    e.drain()
                    e.tensor_reduce(rn[:, qi * 16:(qi + 1) * 16].rearrange("p (h u) -> p h u", u=1),
                                    sqs.rearrange("p (h e) -> p h e", e=DH), AX.X, ALU.add)
                    e.drain()
                e.tensor_scalar_max(rn[:, 0:32], rn[:, 0:32], 1e-24)
                milestone(e, s_v)
            @B.scalar
            def _(e):
                s_v.wait(e)
                e.activation(rn[:, 32:64], rn[:, 0:32], AF.Ln)
                e.drain()
                e.activation(rn[:, 32:64], rn[:, 32:64], AF.Exp, scale=K.cf(-0.5))
                e.drain()
                e.nop().then_inc(s_a.h, 1); s_a.n += 1
            @B.vector
            def _(e):
                s_a.wait(e)
                e.tensor_scalar_min(rn[:, 32:64], rn[:, 32:64], 1e12)
                e.drain()
                e.tensor_scalar_mul(rn[:, 0:16], rn[:, 32:48], 8.0)
                e.drain()
                tq = tabs[:, 0 * TL + tt * DH: 0 * TL + (tt + 1) * DH].rearrange("p (a e) -> p a e", a=1)
                sq_ = tabs[:, 1 * TL + tt * DH: 1 * TL + (tt + 1) * DH].rearrange("p (a s e) -> p a s e", a=1, s=2)
                qv = qkT[:, 0:D].rearrange("p (h e) -> p h e", e=DH)
                qsw = qkT[:, 0:D].rearrange("p (h s e) -> p h s e", s=2, e=32)[:, :, ::-1, :]
                a3, t3 = broadcast_tensor_aps(qv, tq)
                e.tensor_tensor(rot1.rearrange("p (h e) -> p h e", e=DH), a3, t3, ALU.mult)
                a4, t4 = broadcast_tensor_aps(qsw, sq_)
                e.tensor_tensor(rot2.rearrange("p (h s e) -> p h s e", s=2, e=32), a4, t4, ALU.mult)
                e.drain()
                e.tensor_add(qrot, rot1, rot2)
                e.drain()
                tk = tabs[:, 2 * TL + tt * DH: 2 * TL + (tt + 1) * DH].rearrange("p (a e) -> p a e", a=1)
                sk_ = tabs[:, 3 * TL + tt * DH: 3 * TL + (tt + 1) * DH].rearrange("p (a s e) -> p a s e", a=1, s=2)
                kv_ = qkT[:, D:2 * D].rearrange("p (h e) -> p h e", e=DH)
                ksw = qkT[:, D:2 * D].rearrange("p (h s e) -> p h s e", s=2, e=32)[:, :, ::-1, :]
                a5, t5 = broadcast_tensor_aps(kv_, tk)
                e.tensor_tensor(rot1.rearrange("p (h e) -> p h e", e=DH), a5, t5, ALU.mult)
                a6, t6 = broadcast_tensor_aps(ksw, sk_)
                e.tensor_tensor(rot2.rearrange("p (h s e) -> p h s e", s=2, e=32), a6, t6, ALU.mult)
                e.drain()
                e.tensor_add(rot1, rot1, rot2)
                e.drain()
                for h in range(H):
                    e.tensor_scalar_mul(krot[:, h * DH:(h + 1) * DH],
                                        rot1[:, h * DH:(h + 1) * DH], rn[:, 48 + h:49 + h])
                milestone(e, s_v)
        t0 = s_t.n; v0 = s_v.n
        with nc.Block() as B:
            @B.tensor
            def _(e):
                s_v.wait(e)
                for j, srcb in enumerate((qrot, krot)):
                    for cc in range(NC):
                        i = j * NC + cc
                        if i >= 1:
                            e.wait_ge(s_v.h, v0 + i + 1)
                        e.transpose(ptr,
                                    srcb[:, cc * 128:(cc + 1) * 128], idtb)
                        e.drain()
                        e.nop().then_inc(s_t.h, 1)
                s_t.n += 2 * NC
            @B.vector
            def _(e):
                e.tensor_copy(vring[:, (tt % 3) * D:(tt % 3) * D + D], vstg)
                e.drain()
                e.nop().then_inc(s_v.h, 1)
                for j in range(2):
                    for cc in range(NC):
                        i = j * NC + cc
                        e.wait_ge(s_t.h, t0 + i + 1)
                        if j == 0:
                            dst = qrotF[:, cc * 128:(cc + 1) * 128]
                        else:
                            dst = kring[:, (tt % 3) * D + cc * 128:(tt % 3) * D + (cc + 1) * 128]
                        e.tensor_copy(dst, ptr)
                        e.drain()
                        e.nop().then_inc(s_v.h, 1)
                s_v.n += 2 * NC + 1
        K.barrier()
        for h in range(H):
            hc, par = divmod(h, 2)
            t0 = s_t.n; v0 = s_v.n; a0 = s_a.n
            with nc.Block() as B:
                @B.tensor
                def _(e):
                    e.wait_ge(s_v.h, v0)
                    for b in range(nj):
                        sl = slots[b]
                        e.matmul(psim[:, b * 128:(b + 1) * 128],
                                 qrotF[64 * par:64 * par + 64, hc * 128:(hc + 1) * 128],
                                 kring[64 * par:64 * par + 64, sl * D + hc * 128: sl * D + (hc + 1) * 128],
                                 start=True, stop=True)
                    e.drain()
                    e.nop().then_inc(s_t.h, 1); s_t.n += 1
                @B.vector
                def _(e):
                    e.wait_ge(s_a.h, a0)
                    e.wait_ge(s_t.h, t0 + 1)
                    e.scalar_tensor_tensor(simsb[:, 0:W], psim[:, 0:W],
                                           rn[:, h:h + 1],
                                           msk, ALU.mult, ALU.add)
                    e.drain()
                    e.reduce_max(smx[:, 0:1], simsb[:, 0:W], AX.X, negate=True)
                    e.drain()
                    e.nop().then_inc(s_v.h, 1); s_v.n += 1
                @B.scalar
                def _(e):
                    e.wait_ge(s_v.h, v0 + 1)
                    e.activation(psb[:, 0:W], simsb[:, 0:W], AF.Exp,
                                 bias=smx[:, 0:1], accum_out=smx[:, 1:2])
                    e.drain()
                    e.nop().then_inc(s_a.h, 1); s_a.n += 1
                @B.vector
                def _(e):
                    e.wait_ge(s_a.h, a0 + 1)
                    e.reciprocal(smx[:, 2:3], smx[:, 1:2])
                    e.drain()
                    e.tensor_scalar_mul(p2sb[:, 0:W], psb[:, 0:W], smx[:, 2:3])
                    e.drain()
                    e.nop().then_inc(s_v.h, 1); s_v.n += 1
                @B.tensor
                def _(e):
                    e.wait_ge(s_v.h, v0 + 2)
                    for b in range(nj):
                        e.transpose(ppt[:, b * 128:(b + 1) * 128],
                                    p2sb[:, b * 128:(b + 1) * 128], idtb)
                    e.drain()
                    e.nop().then_inc(s_t.h, 1); s_t.n += 1
                @B.vector
                def _(e):
                    e.wait_ge(s_t.h, t0 + 2)
                    e.tensor_copy(pTs[:, 0:W], ppt[:, 0:W])
                    e.drain()
                    e.nop().then_inc(s_v.h, 1); s_v.n += 1
                @B.tensor
                def _(e):
                    e.wait_ge(s_v.h, v0 + 3)
                    for b in range(nj):
                        sl = slots[b]
                        e.matmul(ppv[64 * par:64 * par + 64, :],
                                 vring[:, sl * D + h * DH: sl * D + (h + 1) * DH],
                                 pTs[:, b * 128:(b + 1) * 128],
                                 start=(b == 0), stop=(b == nj - 1),
                                 tile_position=(0, 64 * par))
                    e.drain()
                    e.nop().then_inc(s_t.h, 1); s_t.n += 1
                if par == 1:
                    @B.vector
                    def _(e):
                        e.wait_ge(s_t.h, t0 + 3)
                        e.tensor_copy(aoF[:, hc * 512 + (tt % 4) * 128: hc * 512 + (tt % 4) * 128 + 128],
                                      ppv)
                        e.drain()
                        e.nop().then_inc(s_v.h, 1); s_v.n += 1
        if tt % 4 == 3:
            tb = tt - 3
            wl0 = [s_wl[0].n, s_wl[1].n]
            yl0 = [s_yl[0].n, s_yl[1].n]
            st0 = [s_st[0].n, s_st[1].n]
            t0 = s_t.n; v0 = s_v.n
            with nc.Block() as B:
                @B.gpsimd
                def _(e):
                    for cc in range(NC):
                        if cc >= 2:
                            e.wait_ge(s_t.h, t0 + cc - 1)
                        e.dma_start(out=wos[:, (cc % 2) * D:(cc % 2) * D + D].rearrange("p (a n) -> p a n", a=NC),
                                    in_=Wo[li].rearrange("(a p) n -> p a n", p=128)[:, :, cc * 128:(cc + 1) * 128]
                                    ).then_inc(s_wl[cc % 2].h, 16)
                        s_wl[cc % 2].n += 16
                @B.tensor
                def _(e):
                    for cc in range(NC):
                        e.wait_ge(s_wl[cc % 2].h, wl0[cc % 2] + 16 * (cc // 2 + 1))
                        if cc >= 2:
                            e.wait_ge(s_v.h, v0 + cc - 1)
                        for dd in range(NC):
                            e.matmul(pwo[cc % 2],
                                     wos[:, (cc % 2) * D + dd * 128:(cc % 2) * D + (dd + 1) * 128],
                                     aoF[:, dd * 512:(dd + 1) * 512],
                                     start=(dd == 0), stop=(dd == NC - 1))
                        e.drain()
                        e.nop().then_inc(s_t.h, 1)
                    s_t.n += NC
                @B.sync
                def _(e):
                    s_st[0].wait(e)
                    s_st[1].wait(e)
                    for cc in range(NC):
                        e.dma_start(out=ybuf[:, cc * 512:(cc + 1) * 512],
                                    in_=yFi[cc, :, tb * 128: tb * 128 + 512]).then_inc(s_yl[0].h, 16)
                        s_yl[0].n += 16
                @B.vector
                def _(e):
                    for cc in range(NC):
                        e.wait_ge(s_t.h, t0 + cc + 1)
                        e.wait_ge(s_yl[0].h, yl0[0] + 16 * NC)
                        e.tensor_add(ybuf[:, cc * 512:(cc + 1) * 512],
                                     ybuf[:, cc * 512:(cc + 1) * 512], pwo[cc % 2])
                        e.drain()
                        e.nop().then_inc(s_v.h, 1)
                    s_v.n += NC
                @B.gpsimd
                def _(e):
                    for cc in range(NC):
                        e.wait_ge(s_v.h, v0 + cc + 1)
                        e.dma_start(out=yMid[cc, :, tb * 128: tb * 128 + 512],
                                    in_=ybuf[:, cc * 512:(cc + 1) * 512]
                                    ).then_inc(s_st[cc % 2].h, 16)
                        s_st[cc % 2].n += 16
            K.barrier()

    esA.close()
    yF = alloc("yF", [128, NC * S], F32)
    with nc.Block() as B:
        @B.sync
        def _(e):
            s_st[0].wait(e)
            s_st[1].wait(e)
            e.dma_start(out=yF.rearrange("p (c t) -> p c t", c=NC),
                        in_=yMid.rearrange("c p t -> p c t")).then_inc(s_ld.h, 16)
            s_ld.n += 16
    K.barrier()
    with ExitStack() as es3:
        brow = es3.enter_context(nc.sbuf_tensor(Lp + "brow", [8, 3 * 128], F32)).ap()
        b1row = es3.enter_context(nc.sbuf_tensor(Lp + "b1row", [1, 5460], F32)).ap()
        psf = es3.enter_context(nc.psum_tensor(Lp + "psf", [128, 512], F32)).ap()
        t0 = s_t.n; v0 = s_v.n
        with nc.Block() as B:
            @B.sync
            def _(e):
                for i, vec in enumerate((ln_g[li], ln_b[li], b2[li])):
                    e.dma_start(out=brow[:, i * 128:(i + 1) * 128],
                                in_=vec.rearrange("(a p) -> a p", a=8)).then_inc(s_ld.h, 16)
                    s_ld.n += 16
                e.dma_start(out=b1row, in_=b1[li].rearrange("(a n) -> a n", a=1)).then_inc(s_ld.h, 16)
                s_ld.n += 16
            @B.tensor
            def _(e):
                s_ld.wait(e)
                for i in range(3):
                    e.transpose(psf[:, i * 128: i * 128 + 8],
                                brow[:, i * 128:(i + 1) * 128], idt[0:8, 0:8])
                e.drain()
                e.nop().then_inc(s_t.h, 1); s_t.n += 1
            @B.vector
            def _(e):
                e.wait_ge(s_t.h, t0 + 1)
                for i in range(3):
                    for cc in range(NC):
                        e.tensor_copy(fvec[:, 8 * i + cc: 8 * i + cc + 1],
                                      psf[:, i * 128 + cc: i * 128 + cc + 1])
                e.drain()
                e.nop().then_inc(s_v.h, 1); s_v.n += 1
            @B.tensor
            def _(e):
                e.wait_ge(s_v.h, v0 + 1)
                for hc in range(22):
                    w = 128 if hc < 21 else 42
                    e.transpose(psf[0:w, hc:hc + 1],
                                b1row[:, hc * 128: hc * 128 + w], idt[0:1, 0:1])
                    e.transpose(psf[0:w, 128 + hc:128 + hc + 1],
                                b1row[:, 2730 + hc * 128: 2730 + hc * 128 + w], idt[0:1, 0:1])
                    if hc % 4 == 3:
                        e.drain()
                e.drain()
                e.nop().then_inc(s_t.h, 1); s_t.n += 1
            @B.vector
            def _(e):
                e.wait_ge(s_t.h, t0 + 2)
                e.memset(fvec[:, 24:68], 0.0)
                e.drain()
                for hc in range(22):
                    w = 128 if hc < 21 else 42
                    e.tensor_copy(fvec[0:w, 24 + hc:25 + hc], psf[0:w, hc:hc + 1])
                    e.tensor_copy(fvec[0:w, 46 + hc:47 + hc], psf[0:w, 128 + hc:128 + hc + 1])
                e.drain()
                e.nop().then_inc(s_v.h, 1); s_v.n += 1
    K.barrier()

    hst = alloc("hst", [128, 22 * 512], BF16)
    mub = alloc("mub", [128, 512], F32)
    msqb = alloc("msqb", [128, 512], F32)
    varb = alloc("varb", [128, 512], F32)
    rstdb = alloc("rstdb", [128, 512], F32)
    sqbuf = alloc("sqbuf", [128, 512], F32)
    gel = alloc("gel", [128, 512], F32)
    tba = alloc("tba", [128, 512], F32)
    w1buf = alloc("w1buf", [128, 2 * NC * 256], BF16)
    w2all = alloc("w2all", [128, 22 * 1024], BF16)
    pS1 = palloc("pS1", [128, 512], F32)
    pQ1 = palloc("pQ1", [128, 512], F32)
    pA = palloc("pA", [128, 512], F32)
    pG = palloc("pG", [128, 512], F32)
    pW2 = [palloc("pW20", [128, 512], F32), palloc("pW21", [128, 512], F32)]

    with nc.Block() as B:
        @B.vector
        def _(e):
            e.memset(w2all[32:64, 21 * 1024: 22 * 1024], 0.0)
            e.memset(w2all[64:128, 21 * 1024: 22 * 1024], 0.0)
            e.memset(hst[:, 21 * 512: 22 * 512], 0.0)
            e.drain()
            e.nop().then_inc(s_v.h, 1); s_v.n += 1
        @B.gpsimd
        def _(e):
            s_v.wait(e)
            for hc in range(22):
                w = 128 if hc < 21 else 42
                e.dma_start(out=w2all[0:w, hc * 1024: hc * 1024 + 1024],
                            in_=W2[li][hc * 128: hc * 128 + w, :]).then_inc(s_gld.h, 16)
                s_gld.n += 16
    K.barrier()

    def ys2_ap(dd, lo, hi):
        if dd < 4:
            return yF[:, dd * S + lo: dd * S + hi]
        return yF[:, dd * S + lo - 1: dd * S + hi - 1]

    for ts in range(4):
        lo, hi = ts * 512, (ts + 1) * 512
        edge = (ts == 0)
        t0 = s_t.n; v0 = s_v.n; a0 = s_a.n; g0 = s_g.n
        with nc.Block() as B:
            @B.vector
            def _(e):
                for dd in range(NC):
                    lo2 = lo + 1 if (edge and dd >= 4) else lo
                    width = hi - lo2
                    if dd > 0:
                        e.wait_ge(s_t.h, t0 + dd)
                    e.tensor_mul(sqbuf[:, 0:width], ys2_ap(dd, lo2, hi), ys2_ap(dd, lo2, hi))
                    if edge and dd >= 4:
                        e.memset(sqbuf[:, 511:512], 0.0)
                    e.drain()
                    e.nop().then_inc(s_v.h, 1)
                s_v.n += NC
            @B.tensor
            def _(e):
                for dd in range(NC):
                    lo2 = lo + 1 if (edge and dd >= 4) else lo
                    width = hi - lo2
                    coff = 512 - width
                    e.matmul(pS1[:, coff:512], ones, ys2_ap(dd, lo2, hi),
                             start=(dd == 0), stop=(dd == NC - 1))
                    e.wait_ge(s_v.h, v0 + dd + 1)
                    e.matmul(pQ1[:, coff:512], ones, sqbuf[:, 0:width],
                             start=(dd == 0), stop=(dd == NC - 1))
                    e.drain()
                    e.nop().then_inc(s_t.h, 1)
                s_t.n += NC
            @B.vector
            def _(e):
                e.wait_ge(s_t.h, t0 + NC)
                e.tensor_scalar_mul(mub, pS1, 1.0 / D)
                e.drain()
                e.tensor_mul(msqb, mub, mub)
                e.drain()
                e.scalar_tensor_tensor(varb, pQ1, 1.0 / D, msqb, ALU.mult, ALU.subtract)
                e.drain()
                e.nop().then_inc(s_v.h, 1); s_v.n += 1
            @B.scalar
            def _(e):
                s_v.wait(e)
                e.activation(varb, varb, AF.Ln, bias=K.cf(1e-5))
                e.drain()
                e.activation(rstdb, varb, AF.Exp, scale=K.cf(-0.5))
                e.drain()
                e.nop().then_inc(s_a.h, 1); s_a.n += 1
        with nc.Block() as B:
            @B.vector
            def _(e):
                s_a.wait(e)
                for dd in range(NC):
                    lo2 = lo + 1 if (edge and dd >= 4) else lo
                    width = hi - lo2
                    coff = 512 - width
                    e.tensor_sub(tba[:, coff:512], ys2_ap(dd, lo2, hi), mub[:, coff:512])
                    e.drain()
                    e.tensor_mul(tba[:, coff:512], tba[:, coff:512], rstdb[:, coff:512])
                    e.drain()
                    e.tensor_scalar(ysb[:, dd * S + lo2: dd * S + hi], tba[:, coff:512],
                                    fvec[:, dd:dd + 1], fvec[:, 8 + dd:9 + dd],
                                    ALU.mult, ALU.add)
                    e.drain()
                    if edge and dd >= 4:
                        e.tensor_mul(tba[:, 0:1], mub[:, 0:1], rstdb[:, 0:1])
                        e.drain()
                        e.tensor_scalar_mul(tba[:, 1:2], tba[:, 0:1], -1.0)
                        e.drain()
                        e.tensor_scalar(ysb[:, dd * S + lo: dd * S + lo + 1], tba[:, 1:2],
                                        fvec[:, dd:dd + 1], fvec[:, 8 + dd:9 + dd],
                                        ALU.mult, ALU.add)
                        e.drain()
                milestone(e, s_v)
        K.barrier()

    for ts in range(4):
        lo, hi = ts * 512, (ts + 1) * 512
        wl0 = [s_wl[0].n, s_wl[1].n]
        t0 = s_t.n; v0 = s_v.n; a0 = s_a.n
        with nc.Block() as B:
            @B.gpsimd
            def _(e):
                for hc in range(22):
                    w = 128 if hc < 21 else 42
                    if hc >= 2:
                        e.wait_ge(s_t.h, t0 + hc - 1)
                    base = (hc % 2) * NC * 256
                    e.dma_start(out=w1buf[:, base: base + NC * 256].rearrange("p (c n) -> p c n", c=NC)[:, :, 0:w],
                                in_=W1[li].rearrange("(c p) n -> p c n", p=128)[:, :, hc * 128: hc * 128 + w]
                                ).then_inc(s_wl[hc % 2].h, 16)
                    s_wl[hc % 2].n += 16
                    e.dma_start(out=w1buf[:, base: base + NC * 256].rearrange("p (c n) -> p c n", c=NC)[:, :, 128:128 + w],
                                in_=W1[li].rearrange("(c p) n -> p c n", p=128)[:, :, 2730 + hc * 128: 2730 + hc * 128 + w]
                                ).then_inc(s_wl[hc % 2].h, 16)
                    s_wl[hc % 2].n += 16
            @B.tensor
            def _(e):
                for hc in range(22):
                    w = 128 if hc < 21 else 42
                    base = (hc % 2) * NC * 256
                    e.wait_ge(s_wl[hc % 2].h, wl0[hc % 2] + 32 * (hc // 2 + 1))
                    if hc >= 1:
                        e.wait_ge(s_v.h, v0 + hc)
                    for dd in range(NC):
                        e.matmul(pA[0:w, :],
                                 w1buf[:, base + dd * 256: base + dd * 256 + w],
                                 ysb[:, dd * S + lo: dd * S + hi],
                                 start=(dd == 0), stop=(dd == NC - 1))
                    for dd in range(NC):
                        e.matmul(pG[0:w, :],
                                 w1buf[:, base + dd * 256 + 128: base + dd * 256 + 128 + w],
                                 ysb[:, dd * S + lo: dd * S + hi],
                                 start=(dd == 0), stop=(dd == NC - 1))
                    e.drain()
                    e.nop().then_inc(s_t.h, 1)
                s_t.n += 22
            @B.scalar
            def _(e):
                for hc in range(22):
                    w = 128 if hc < 21 else 42
                    e.wait_ge(s_t.h, t0 + hc + 1)
                    if hc >= 1:
                        e.wait_ge(s_v.h, v0 + hc)
                    e.activation(gel[0:w, :], pG[0:w, :], AF.Gelu,
                                 bias=fvec[0:w, 46 + hc:47 + hc])
                    e.drain()
                    e.nop().then_inc(s_a.h, 1)
                s_a.n += 22
            @B.vector
            def _(e):
                for hc in range(22):
                    w = 128 if hc < 21 else 42
                    e.wait_ge(s_a.h, a0 + hc + 1)
                    e.scalar_tensor_tensor(hst[0:w, hc * 512:(hc + 1) * 512],
                                           pA[0:w, :], fvec[0:w, 24 + hc:25 + hc],
                                           gel[0:w, :], ALU.add, ALU.mult)
                    e.drain()
                    e.nop().then_inc(s_v.h, 1)
                s_v.n += 22
        K.barrier()
        t0 = s_t.n; v0 = s_v.n
        with nc.Block() as B:
            @B.tensor
            def _(e):
                s_gld.wait(e)
                for oc in range(NC):
                    if oc >= 2:
                        e.wait_ge(s_v.h, v0 + oc - 1)
                    for hc in range(22):
                        e.matmul(pW2[oc % 2],
                                 w2all[:, hc * 1024 + oc * 128: hc * 1024 + oc * 128 + 128],
                                 hst[:, hc * 512:(hc + 1) * 512],
                                 start=(hc == 0), stop=(hc == 21))
                    e.drain()
                    e.nop().then_inc(s_t.h, 1)
                s_t.n += NC
            @B.vector
            def _(e):
                for oc in range(NC):
                    e.wait_ge(s_t.h, t0 + oc + 1)
                    ysl = yF[:, oc * S + lo: oc * S + hi]
                    e.scalar_tensor_tensor(ysl, pW2[oc % 2], fvec[:, 16 + oc:17 + oc],
                                           ysl, ALU.add, ALU.add)
                    e.drain()
                    e.nop().then_inc(s_v.h, 1)
                s_v.n += NC
        K.barrier()

    s_fo = K.sem(Lp + "fo")
    with nc.Block() as B:
        @B.sync
        def _(e):
            e.dma_start(out=yFo.rearrange("c p t -> p c t"),
                        in_=yF.rearrange("p (c t) -> p c t", c=NC)).then_inc(s_fo.h, 16)
            s_fo.n += 16
            s_fo.wait(e)
    K.barrier()
    es.close()



def stage_mid(K, accF, yF, ones128, x1F, xn2Fb):
    nc = K.nc
    Lp = "m_"
    s_ld = K.sem(Lp + "ld")
    s_t = K.sem(Lp + "t")
    s_v = K.sem(Lp + "v")
    s_a = K.sem(Lp + "a")
    s_st = K.sem(Lp + "st")
    es = ExitStack()
    alloc = lambda n, sh, dt: es.enter_context(nc.sbuf_tensor(Lp + n, sh, dt)).ap()
    x1 = alloc("x1", [128, NC * S], F32)
    yb = alloc("yb", [128, NC * S], F32)
    xnb = alloc("xnb", [128, NC * S], BF16)
    ones = alloc("ones", [128, 128], F32)
    sqb = alloc("sqb", [128, 512], F32)
    mub = alloc("mub", [128, 512], F32)
    msq = alloc("msq", [128, 512], F32)
    varb = alloc("varb", [128, 512], F32)
    rstdb = alloc("rstdb", [128, 512], F32)
    with (
        nc.psum_tensor(Lp + "pS", [128, 512], F32) as pS_,
        nc.psum_tensor(Lp + "pQ", [128, 512], F32) as pQ_,
    ):
        pS, pQ = pS_.ap(), pQ_.ap()
        with nc.Block() as B:
            @B.sync
            def _(e):
                e.dma_start(out=ones, in_=ones128).then_inc(s_ld.h, 16); s_ld.n += 16
                e.dma_start(out=x1.rearrange("p (c t) -> p c t", c=NC),
                            in_=accF.rearrange("c p t -> p c t")).then_inc(s_ld.h, 16)
                s_ld.n += 16
                e.dma_start(out=yb.rearrange("p (c t) -> p c t", c=NC),
                            in_=yF.rearrange("c p t -> p c t")).then_inc(s_ld.h, 16)
                s_ld.n += 16
            @B.vector
            def _(e):
                s_ld.wait(e)
                for cc in range(NC):
                    e.tensor_add(x1[:, cc * S:(cc + 1) * S], x1[:, cc * S:(cc + 1) * S],
                                 yb[:, cc * S:(cc + 1) * S])
                milestone(e, s_v)
        K.barrier()
        with nc.Block() as B:
            @B.sync
            def _(e):
                e.dma_start(out=x1F.rearrange("c p t -> p c t"),
                            in_=x1.rearrange("p (c t) -> p c t", c=NC)).then_inc(s_st.h, 16)
                s_st.n += 16
        for ts in range(4):
            lo, hi = ts * 512, (ts + 1) * 512
            t0 = s_t.n; v0 = s_v.n; a0 = s_a.n
            with nc.Block() as B:
                @B.vector
                def _(e):
                    for dd in range(NC):
                        if dd > 0:
                            e.wait_ge(s_t.h, t0 + dd)
                        e.tensor_mul(sqb, x1[:, dd * S + lo: dd * S + hi],
                                     x1[:, dd * S + lo: dd * S + hi])
                        e.drain()
                        e.nop().then_inc(s_v.h, 1)
                    s_v.n += NC
                @B.tensor
                def _(e):
                    for dd in range(NC):
                        e.matmul(pS, ones, x1[:, dd * S + lo: dd * S + hi],
                                 start=(dd == 0), stop=(dd == NC - 1))
                        e.wait_ge(s_v.h, v0 + dd + 1)
                        e.matmul(pQ, ones, sqb, start=(dd == 0), stop=(dd == NC - 1))
                        e.drain()
                        e.nop().then_inc(s_t.h, 1)
                    s_t.n += NC
                @B.vector
                def _(e):
                    e.wait_ge(s_t.h, t0 + NC)
                    e.tensor_scalar_mul(mub, pS, 1.0 / D)
                    e.drain()
                    e.tensor_mul(msq, mub, mub)
                    e.drain()
                    e.scalar_tensor_tensor(varb, pQ, 1.0 / D, msq, ALU.mult, ALU.subtract)
                    e.drain()
                    e.nop().then_inc(s_v.h, 1); s_v.n += 1
                @B.scalar
                def _(e):
                    s_v.wait(e)
                    e.activation(varb, varb, AF.Ln, bias=K.cf(1e-5))
                    e.drain()
                    e.activation(rstdb, varb, AF.Exp, scale=K.cf(-0.5))
                    e.drain()
                    e.nop().then_inc(s_a.h, 1); s_a.n += 1
                @B.vector
                def _(e):
                    s_a.wait(e)
                    for dd in range(NC):
                        e.tensor_sub(sqb, x1[:, dd * S + lo: dd * S + hi], mub)
                        e.drain()
                        e.tensor_mul(sqb, sqb, rstdb)
                        e.drain()
                        e.tensor_copy(xnb[:, dd * S + lo: dd * S + hi], sqb)
                        e.drain()
                    milestone(e, s_v)
            K.barrier()
        with nc.Block() as B:
            @B.sync
            def _(e):
                e.dma_start(out=xn2Fb.rearrange("c p t -> p c t"),
                            in_=xnb.rearrange("p (c t) -> p c t", c=NC)).then_inc(s_st.h, 16)
                s_st.n += 16
                s_st.wait(e)
    K.barrier()
    es.close()



def stage_l1attn(K, xn2Fb, W_sel, dt_W, dt_b, Wk1, Wv1, Wr1, Wg1, Wo1,
                 base_decay, time_first, ones128, ident, tri16x8, bsel128, a2F):
    nc = K.nc
    Lp = "g_"
    NH = 8
    DH = 64
    IN = 512
    NM = 4
    s_gld = K.sem(Lp + "gld")
    s_ld = K.sem(Lp + "ld")
    s_t = K.sem(Lp + "t")
    s_v = K.sem(Lp + "v")
    s_a = K.sem(Lp + "a")
    s_st = [K.sem(Lp + "st0"), K.sem(Lp + "st1")]
    es = ExitStack()
    alloc = lambda n, sh, dt: es.enter_context(nc.sbuf_tensor(Lp + n, sh, dt)).ap()
    palloc = lambda n, sh, dt: es.enter_context(nc.psum_tensor(Lp + n, sh, dt)).ap()

    idt = alloc("idt", [128, 128], F32)
    idtb = alloc("idtb", [128, 128], BF16)
    ones = alloc("ones", [128, 128], F32)
    tri = alloc("tri", [16, 128], F32)
    bsel = alloc("bsel", [128, 1], F32)
    vecs = alloc("vecs", [128, 16], F32)
    vF = alloc("vF", [128, NM * S], BF16)
    rgF = alloc("rgF", [128, NM * S], BF16)
    GF = alloc("GF", [128, NM * S], BF16)
    qtF = alloc("qtF", [128, NM * S], BF16)
    kEF = alloc("kEF", [128, NM * S], BF16)
    kSF = alloc("kSF", [128, NM * S], BF16)
    Atab = alloc("Atab", [128, NM * 128], F32)
    S_sb = alloc("S_sb", [128, NM * DH], BF16)
    bsc = alloc("bsc", [1, S], F32)

    with ExitStack() as e1:
        a1 = lambda n, sh, dt: e1.enter_context(nc.sbuf_tensor(Lp + n, sh, dt)).ap()
        xnb = a1("xnb", [128, NC * S], BF16)
        kT_ = a1("kTile", [128, 512], BF16)
        bselb = a1("bselb", [128, 1], BF16)
        wsel = a1("wsel", [128, NC * 96], BF16)
        wdt = a1("wdt", [32, 512], BF16)
        wkvm = a1("wkvm", [128, 2 * 4 * NC * 128], BF16)
        vrow = a1("vrow", [4, 3 * 128], F32)
        projb = a1("projb", [128, S], BF16)
        Lcum = a1("L", [128, NM * S], F32)
        sc1 = a1("sc1", [128, 512], F32)
        sc2 = a1("sc2", [128, 512], F32)
        sc3 = a1("sc3", [128, 512], F32)
        pp0 = e1.enter_context(nc.psum_tensor(Lp + "pp0", [128, 512], F32)).ap()
        pp1 = e1.enter_context(nc.psum_tensor(Lp + "pp1", [128, 512], F32)).ap()
        pv = e1.enter_context(nc.psum_tensor(Lp + "pv", [128, 512], F32)).ap()
        pp = [pp0, pp1]

        with nc.Block() as B:
            @B.sync
            def _(e):
                e.dma_start(out=idt, in_=ident).then_inc(s_ld.h, 16); s_ld.n += 16
                e.dma_start(out=ones, in_=ones128).then_inc(s_ld.h, 16); s_ld.n += 16
                e.dma_start(out=tri, in_=tri16x8).then_inc(s_ld.h, 16); s_ld.n += 16
                e.dma_start(out=bsel[0:96, :], in_=bsel128.rearrange("(p a) -> p a", a=1)
                            ).then_inc(s_ld.h, 16); s_ld.n += 16
                e.dma_start(out=xnb.rearrange("p (c t) -> p c t", c=NC),
                            in_=xn2Fb.rearrange("c p t -> p c t")).then_inc(s_ld.h, 16)
                s_ld.n += 16
                e.dma_start(out=vrow[:, 0:128], in_=dt_b.rearrange("(a p) -> a p", a=4)
                            ).then_inc(s_ld.h, 16); s_ld.n += 16
                e.dma_start(out=vrow[:, 128:256],
                            in_=base_decay.rearrange("h d -> (h d)").rearrange("(a p) -> a p", a=4)
                            ).then_inc(s_ld.h, 16); s_ld.n += 16
                e.dma_start(out=vrow[:, 256:384],
                            in_=time_first.rearrange("h d -> (h d)").rearrange("(a p) -> a p", a=4)
                            ).then_inc(s_ld.h, 16); s_ld.n += 16
            @B.gpsimd
            def _(e):
                e.dma_start(out=wsel.rearrange("p (c n) -> p c n", c=NC),
                            in_=W_sel.rearrange("(c p) n -> p c n", p=128)[:, :, 0:96]
                            ).then_inc(s_gld.h, 16); s_gld.n += 16
                e.dma_start(out=wdt, in_=dt_W).then_inc(s_gld.h, 16); s_gld.n += 16
            @B.tensor
            def _(e):
                s_ld.wait(e)
                for i in range(3):
                    e.transpose(pv[:, i * 4: i * 4 + 4], vrow[:, i * 128:(i + 1) * 128],
                                idt[0:4, 0:4])
                e.drain()
                e.nop().then_inc(s_t.h, 1); s_t.n += 1
            @B.vector
            def _(e):
                s_t.wait(e)
                e.tensor_copy(idtb, idt)
                e.tensor_copy(vecs[:, 0:4], pv[:, 0:4])
                e.tensor_scalar_mul(vecs[:, 4:8], pv[:, 4:8], -1.0)
                e.tensor_copy(vecs[:, 8:12], pv[:, 8:12])
                e.memset(vecs[:, 12:16], 1.0)
                e.memset(projb, 0.0)
                e.memset(S_sb, 0.0)
                e.tensor_copy(bselb[0:96, :], bsel[0:96, :])
                milestone(e, s_v)
        K.barrier()

        t0 = s_t.n; v0 = s_v.n
        with nc.Block() as B:
            @B.tensor
            def _(e):
                s_gld.wait(e)
                for ts in range(4):
                    if ts >= 2:
                        e.wait_ge(s_v.h, v0 + 2 * (ts - 1))
                    for dd in range(NC):
                        e.matmul(pp[ts % 2][0:96, :],
                                 wsel[:, dd * 96:(dd + 1) * 96],
                                 xnb[:, dd * S + ts * 512: dd * S + (ts + 1) * 512],
                                 start=(dd == 0), stop=(dd == NC - 1))
                    e.drain()
                    e.nop().then_inc(s_t.h, 1)
                s_t.n += 4
            @B.vector
            def _(e):
                for ts in range(4):
                    e.wait_ge(s_t.h, t0 + ts + 1)
                    e.tensor_copy(projb[0:96, ts * 512:(ts + 1) * 512], pp[ts % 2][0:96, :])
                    e.drain()
                    e.nop().then_inc(s_v.h, 2)
                s_v.n += 8
        K.barrier()
        t0 = s_t.n; v0 = s_v.n
        with nc.Block() as B:
            @B.tensor
            def _(e):
                for ts in range(4):
                    e.matmul(pv[0:1, 0:512], bselb[0:96, :],
                             projb[0:96, ts * 512:(ts + 1) * 512], start=True, stop=True)
                    e.drain()
                    e.nop().then_inc(s_t.h, 1)
                    e.wait_ge(s_v.h, v0 + ts + 1)
                s_t.n += 4
            @B.vector
            def _(e):
                for ts in range(4):
                    e.wait_ge(s_t.h, t0 + ts + 1)
                    e.tensor_copy(bsc[:, ts * 512:(ts + 1) * 512], pv[0:1, 0:512])
                    e.drain()
                    e.nop().then_inc(s_v.h, 1)
                s_v.n += 4
        K.barrier()

        for m in range(NM):
            t0 = s_t.n; v0 = s_v.n; a0 = s_a.n
            with nc.Block() as B:
                @B.tensor
                def _(e):
                    for ts in range(4):
                        if ts >= 2:
                            e.wait_ge(s_v.h, v0 + 4 * (ts - 1))
                        e.matmul(pp[ts % 2], wdt[:, m * 128:(m + 1) * 128],
                                 projb[0:32, ts * 512:(ts + 1) * 512], start=True, stop=True)
                        e.drain()
                        e.nop().then_inc(s_t.h, 1)
                    s_t.n += 4
                @B.vector
                def _(e):
                    for ts in range(4):
                        e.wait_ge(s_t.h, t0 + ts + 1)
                        e.tensor_scalar(sc1, pp[ts % 2], vecs[:, m:m + 1], None, ALU.add)
                        e.drain()
                        e.tensor_scalar_mul(sc3, sc1, -1.0)
                        e.drain()
                        e.tensor_tensor(sc1, sc1, sc3, ALU.max)
                        e.drain()
                        e.nop().then_inc(s_v.h, 1)
                        e.wait_ge(s_a.h, a0 + 3 * ts + 3)
                        e.tensor_add(sc1, sc2, sc3)
                        e.drain()
                        e.tensor_scalar_mul(sc1, sc1, vecs[:, 4 + m:5 + m])
                        e.drain()
                        _, o2b = broadcast_tensor_aps(sc1, vecs[:, 15:16])
                        e.tensor_tensor_scan(
                            Lcum[:, m * S + ts * 512: m * S + (ts + 1) * 512],
                            sc1, o2b, 0.0, ALU.add, ALU.mult)
                        e.drain()
                        e.nop().then_inc(s_v.h, 3)
                    s_v.n += 16
                @B.scalar
                def _(e):
                    for ts in range(4):
                        e.wait_ge(s_v.h, v0 + 4 * ts + 1)
                        e.activation(sc2, sc1, AF.Exp, scale=K.cf(-1.0))
                        e.drain()
                        e.activation(sc2, sc2, AF.Ln, bias=K.cf(1.0))
                        e.drain()
                        e.activation(sc3, pp[ts % 2], AF.Relu, bias=vecs[:, m:m + 1])
                        e.drain()
                        e.nop().then_inc(s_a.h, 3)
                    s_a.n += 12
            K.barrier()
        with nc.Block() as B:
            @B.vector
            def _(e):
                for m in range(NM):
                    for ts in range(1, 4):
                        base = Lcum[:, m * S + ts * 512 - 1: m * S + ts * 512]
                        seg = Lcum[:, m * S + ts * 512: m * S + (ts + 1) * 512]
                        e.tensor_scalar(seg, seg, base, None, ALU.add)
                        e.drain()
                milestone(e, s_v)
        K.barrier()
        s_wm = K.sem(Lp + "wm")
        for m in range(NM):
            with nc.Block() as B:
                @B.gpsimd
                def _(e):
                    base = (m % 2) * 4 * NC * 128
                    for wi, Wm in enumerate((Wk1, Wv1, Wr1, Wg1)):
                        e.dma_start(out=wkvm[:, base + wi * NC * 128: base + (wi + 1) * NC * 128
                                             ].rearrange("p (c n) -> p c n", c=NC),
                                    in_=Wm.rearrange("(c p) n -> p c n", p=128)[:, :, m * 128:(m + 1) * 128]
                                    ).then_inc(s_wm.h, 16)
                        s_wm.n += 16
            K.barrier()
            for ts in range(4):
                t0 = s_t.n; v0 = s_v.n; a0 = s_a.n
                lo = ts * 512
                sl = slice(m * S + lo, m * S + lo + 512)
                with nc.Block() as B:
                    @B.tensor
                    def _(e):
                        s_wm.wait(e)
                        e.wait_ge(s_v.h, v0)
                        for wi in range(4):
                            if wi >= 2:
                                e.wait_ge(s_v.h, v0 + [0, 0, 1, 2][wi])
                            base = (m % 2) * 4 * NC * 128
                            for dd in range(NC):
                                e.matmul(pp[wi % 2],
                                         wkvm[:, base + wi * NC * 128 + dd * 128:
                                               base + wi * NC * 128 + (dd + 1) * 128],
                                         xnb[:, dd * S + lo: dd * S + lo + 512],
                                         start=(dd == 0), stop=(dd == NC - 1))
                            e.drain()
                            e.nop().then_inc(s_t.h, 1)
                        e.matmul(pv, ones[0:1, :], bsc[:, lo: lo + 512], start=True, stop=True)
                        e.drain()
                        e.nop().then_inc(s_t.h, 1)
                        s_t.n += 5
                    @B.vector
                    def _(e):
                        e.wait_ge(s_t.h, t0 + 1)
                        e.tensor_copy(kT_, pp[0]).then_inc(s_v.h, 1)
                        e.drain()
                        e.wait_ge(s_t.h, t0 + 2)
                        e.tensor_copy(vF[:, sl], pp[1]).then_inc(s_v.h, 1)
                        e.drain()
                        e.wait_ge(s_a.h, a0 + 2)
                        e.tensor_mul(rgF[:, sl], sc1, sc2)
                        e.drain()
                        e.nop().then_inc(s_v.h, 1)
                        e.scalar_tensor_tensor(GF[:, sl], kT_, vecs[:, 8 + m:9 + m],
                                               vF[:, sl], ALU.mult, ALU.mult)
                        e.drain()
                        e.nop().then_inc(s_v.h, 1)
                        Ls = Lcum[:, sl]
                        full = Ls.rearrange("p (b w) -> p b w", w=16)
                        if ts == 0:
                            e.tensor_copy(sc3[:, 0:16], Ls[:, 0:16])
                            e.drain()
                            f2_ = Ls[:, 16:512].rearrange("p (b w) -> p b w", w=16)
                            r2_ = Lcum[:, m * S: m * S + 512].rearrange(
                                "p (b w) -> p b w", w=16)[:, 0:31, 15:16]
                            s2b, r2b = broadcast_tensor_aps(f2_, r2_)
                            e.tensor_tensor(sc3[:, 16:512].rearrange("p (b w) -> p b w", w=16),
                                            s2b, r2b, ALU.subtract)
                        else:
                            ref_ap = Lcum[:, m * S + lo - 1: m * S + lo - 1 + 512].rearrange(
                                "p (b w) -> p b w", w=16)[:, :, 0:1]
                            sb_, rb_ = broadcast_tensor_aps(full, ref_ap)
                            e.tensor_tensor(sc3.rearrange("p (b w) -> p b w", w=16), sb_, rb_,
                                            ALU.subtract)
                        e.drain()
                        e.nop().then_inc(s_v.h, 1)
                        e.wait_ge(s_a.h, a0 + 3)
                        e.tensor_mul(qtF[:, sl], kT_, sc1)
                        e.drain()
                        e.nop().then_inc(s_v.h, 1)
                        en0_ap = Lcum[:, m * S + lo: m * S + lo + 512].rearrange(
                            "p (b w) -> p b w", w=16)[:, :, 15:16]
                        sb2_, bb2_ = broadcast_tensor_aps(full, en0_ap)
                        e.tensor_tensor(sc3.rearrange("p (b w) -> p b w", w=16), bb2_, sb2_,
                                        ALU.subtract)
                        e.drain()
                        e.nop().then_inc(s_v.h, 1)
                        e.wait_ge(s_t.h, t0 + 5)
                        e.tensor_tensor(sc2, kT_, pv, ALU.mult)
                        e.drain()
                        e.wait_ge(s_a.h, a0 + 4)
                        e.tensor_mul(kEF[:, sl], sc2, sc1)
                        e.drain()
                        e.nop().then_inc(s_v.h, 2)
                        if ts == 0:
                            e.memset(sc3[:, 0:16], 0.0)
                            e.drain()
                            Lm0 = Lcum[:, m * S: m * S + 512].rearrange("p (b w) -> p b w", w=16)
                            en2_ = Lm0[:, 1:32, 15:16]
                            r2_ = Lm0[:, 0:31, 15:16]
                            z2 = sc3[:, 16:512].rearrange("p (b w) -> p b w", w=16)
                            _, r2bb = broadcast_tensor_aps(z2, r2_)
                            _, e2bb = broadcast_tensor_aps(z2, en2_)
                            e.tensor_tensor(z2, r2bb, e2bb, ALU.subtract)
                            e.drain()
                            z0 = sc3[:, 0:16].rearrange("p (b w) -> p b w", b=1)
                            _, l0b = broadcast_tensor_aps(
                                z0, Lcum[:, m * S + 15: m * S + 16].rearrange("p (b w) -> p b w", b=1))
                            e.tensor_scalar(z0, l0b, -1.0, None, ALU.mult)
                        else:
                            ref_ap2 = Lcum[:, m * S + lo - 1: m * S + lo - 1 + 512].rearrange(
                                "p (b w) -> p b w", w=16)[:, :, 0:1]
                            _, rb2_ = broadcast_tensor_aps(full, ref_ap2)
                            e.tensor_tensor(sc3.rearrange("p (b w) -> p b w", w=16), rb2_, bb2_,
                                            ALU.subtract)
                        e.drain()
                        e.tensor_scalar_min(sc3, sc3, 80.0)
                        e.drain()
                        e.nop().then_inc(s_v.h, 2)
                        e.wait_ge(s_a.h, a0 + 5)
                        e.tensor_mul(kSF[:, sl], kEF[:, sl], sc1)
                        e.drain()
                        e.nop().then_inc(s_v.h, 1)
                        s_v.n += 12
                    @B.scalar
                    def _(e):
                        e.wait_ge(s_t.h, t0 + 3)
                        e.activation(sc1, pp[0], AF.Sigmoid)
                        e.drain()
                        e.nop().then_inc(s_a.h, 1)
                        e.wait_ge(s_t.h, t0 + 4)
                        e.activation(sc2, pp[1], AF.Sigmoid)
                        e.drain()
                        e.nop().then_inc(s_a.h, 1)
                        for j in range(3):
                            e.wait_ge(s_v.h, v0 + [5, 7, 11][j])
                            e.activation(sc1, sc3, AF.Exp)
                            e.drain()
                            e.nop().then_inc(s_a.h, 1)
                        s_a.n += 5
                K.barrier()
        t0 = s_t.n; v0 = s_v.n; a0 = s_a.n
        with nc.Block() as B:
            @B.vector
            def _(e):
                for m in range(NM):
                    Lm = Lcum[:, m * S:(m + 1) * S].rearrange("p (b w) -> p b w", w=16)
                    e.tensor_copy(Atab[:, m * 128: m * 128 + 1], Lcum[:, m * S + 15: m * S + 16])
                    e.drain()
                    e.tensor_tensor(Atab[:, m * 128 + 1:(m + 1) * 128].rearrange(
                                        "p (b w) -> p b w", w=1),
                                    Lm[:, 1:128, 15:16], Lm[:, 0:127, 15:16], ALU.subtract)
                    e.drain()
                milestone(e, s_v)
            @B.scalar
            def _(e):
                s_v.wait(e)
                e.activation(Atab, Atab, AF.Exp)
                e.drain()
                e.nop().then_inc(s_a.h, 1); s_a.n += 1
        K.barrier()

    outF = es.enter_context(nc.sbuf_tensor(Lp + "outF", [128, NM * S], BF16)).ap()
    with ExitStack() as e2:
        a2_ = lambda n, sh, dt: e2.enter_context(nc.sbuf_tensor(Lp + n, sh, dt)).ap()
        k16 = a2_("k16", [16, 2 * 8 * IN], BF16)
        v16 = a2_("v16", [16, 2 * 8 * IN], BF16)
        PmD = a2_("PmD", [16, 8 * 128], BF16)
        mrg = a2_("mrg", [128, 128], F32)
        ptr16 = e2.enter_context(nc.psum_tensor(Lp + "ptr16", [16, 128], BF16)).ap()
        ptr16b = e2.enter_context(nc.psum_tensor(Lp + "ptr16b", [16, 128], BF16)).ap()
        psP = e2.enter_context(nc.psum_tensor(Lp + "psP", [16, 8 * 128], F32)).ap()
        psO = e2.enter_context(nc.psum_tensor(Lp + "psO", [128, 4 * 128], F32)).ap()
        psI = e2.enter_context(nc.psum_tensor(Lp + "psI", [128, 4 * 128], F32)).ap()
        psU = e2.enter_context(nc.psum_tensor(Lp + "psU", [128, 4 * 64], F32)).ap()
        s_tr = K.sem(Lp + "tr")
        s_te = K.sem(Lp + "te")

        for g in range(NT):
            slot = (g % 2) * 8 * IN
            gl = g * 128
            tr0 = s_tr.n; te0 = s_te.n
            with nc.Block() as B:
                @B.tensor
                def _(e):
                    for b2 in range(8):
                        for m in range(NM):
                            i = b2 * NM + m
                            if i >= 1:
                                e.wait_ge(s_te.h, te0 + 2 * i)
                            e.transpose(ptr16,
                                        kEF[:, m * S + gl + b2 * 16: m * S + gl + b2 * 16 + 16],
                                        idtb)
                            e.transpose(ptr16b,
                                        vF[:, m * S + gl + b2 * 16: m * S + gl + b2 * 16 + 16],
                                        idtb)
                            e.drain()
                            e.nop().then_inc(s_tr.h, 1)
                    s_tr.n += 32
                @B.vector
                def _(e):
                    for b2 in range(8):
                        for m in range(NM):
                            i = b2 * NM + m
                            e.wait_ge(s_tr.h, tr0 + i + 1)
                            e.tensor_copy(k16[:, slot + b2 * IN + m * 128:
                                             slot + b2 * IN + (m + 1) * 128],
                                          ptr16)
                            e.tensor_copy(v16[:, slot + b2 * IN + m * 128:
                                             slot + b2 * IN + (m + 1) * 128],
                                          ptr16b)
                            e.drain()
                            e.nop().then_inc(s_te.h, 2)
                    s_te.n += 64
            K.barrier()
            t0 = s_t.n; v0 = s_v.n
            with nc.Block() as B:
                @B.tensor
                def _(e):
                    for h in range(NH):
                        hp, par = divmod(h, 2)
                        hcol = slice(64 * par, 64 * par + 64)
                        for b2 in range(8):
                            e.matmul(psP[:, h * 128 + b2 * 16: h * 128 + (b2 + 1) * 16],
                                     kSF[hcol, hp * S + gl + b2 * 16: hp * S + gl + b2 * 16 + 16],
                                     qtF[hcol, hp * S + gl + b2 * 16: hp * S + gl + b2 * 16 + 16],
                                     start=True, stop=True)
                        e.drain()
                    e.nop().then_inc(s_t.h, 1); s_t.n += 1
                @B.vector
                def _(e):
                    e.wait_ge(s_t.h, t0 + 1)
                    tri3 = tri.rearrange("p (a n) -> p a n", a=1)
                    for hb in range(2):
                        psP3 = psP[:, hb * 512:(hb + 1) * 512].rearrange("p (h n) -> p h n", h=NH // 2)
                        t3, m3 = broadcast_tensor_aps(psP3, tri3)
                        e.tensor_tensor(PmD[:, hb * 512:(hb + 1) * 512].rearrange("p (h n) -> p h n", h=NH // 2),
                                        t3, m3, ALU.mult)
                        e.drain()
                    e.nop().then_inc(s_v.h, 1); s_v.n += 1
            K.barrier()
            for b2 in range(8):
                t0 = s_t.n; v0 = s_v.n
                with nc.Block() as B:
                    @B.tensor
                    def _(e):
                        for h in range(NH):
                            hp, par = divmod(h, 2)
                            hcol = slice(64 * par, 64 * par + 64)
                            tp = (0, 64 * par)
                            ocol = slice(hp * 128 + b2 * 16, hp * 128 + (b2 + 1) * 16)
                            e.matmul(psO[64 * par:64 * par + 64, ocol],
                                     v16[:, slot + b2 * IN + h * DH: slot + b2 * IN + (h + 1) * DH],
                                     PmD[:, h * 128 + b2 * 16: h * 128 + (b2 + 1) * 16],
                                     start=True, stop=True, tile_position=tp)
                            e.matmul(psI[64 * par:64 * par + 64, ocol],
                                     S_sb[hcol, hp * DH:(hp + 1) * DH],
                                     qtF[hcol, hp * S + gl + b2 * 16: hp * S + gl + b2 * 16 + 16],
                                     start=True, stop=True,
                                     tile_position=(64 * par, 64 * par))
                            e.matmul(psU[64 * par:64 * par + 64, hp * DH:(hp + 1) * DH],
                                     k16[:, slot + b2 * IN + h * DH: slot + b2 * IN + (h + 1) * DH],
                                     v16[:, slot + b2 * IN + h * DH: slot + b2 * IN + (h + 1) * DH],
                                     start=True, stop=True, tile_position=tp)
                            if h % 2 == 1:
                                e.drain()
                        e.drain()
                        e.nop().then_inc(s_t.h, 1); s_t.n += 1
                    @B.vector
                    def _(e):
                        e.wait_ge(s_t.h, t0 + 1)
                        ci = g * 8 + b2
                        for hp in range(4):
                            e.scalar_tensor_tensor(S_sb[:, hp * DH:(hp + 1) * DH],
                                                   S_sb[:, hp * DH:(hp + 1) * DH],
                                                   Atab[:, hp * 128 + ci: hp * 128 + ci + 1],
                                                   psU[:, hp * DH:(hp + 1) * DH],
                                                   ALU.mult, ALU.add)
                            e.drain()
                        e.nop().then_inc(s_v.h, 1); s_v.n += 1
                K.barrier()
            t0 = s_t.n; v0 = s_v.n
            with nc.Block() as B:
                @B.vector
                def _(e):
                    for hp in range(4):
                        e.tensor_add(mrg, psO[:, hp * 128:(hp + 1) * 128],
                                     GF[:, hp * S + gl: hp * S + gl + 128])
                        e.drain()
                        e.tensor_add(mrg, mrg, psI[:, hp * 128:(hp + 1) * 128])
                        e.drain()
                        e.tensor_mul(outF[:, hp * S + gl: hp * S + gl + 128], mrg,
                                     rgF[:, hp * S + gl: hp * S + gl + 128])
                        e.drain()
                    milestone(e, s_v)
            K.barrier()

    with ExitStack() as e3:
        a3_ = lambda n, sh, dt: e3.enter_context(nc.sbuf_tensor(Lp + n, sh, dt)).ap()
        wo1 = a3_("wo1", [128, NM * 1024], BF16)
        ob = a3_("ob", [128, 2 * 512], F32)
        pw = [e3.enter_context(nc.psum_tensor(Lp + "pwA", [128, 512], F32)).ap(),
              e3.enter_context(nc.psum_tensor(Lp + "pwB", [128, 512], F32)).ap()]
        with nc.Block() as B:
            @B.gpsimd
            def _(e):
                e.dma_start(out=wo1.rearrange("p (c n) -> p c n", c=NM),
                            in_=Wo1.rearrange("(c p) n -> p c n", p=128)).then_inc(s_gld.h, 16)
                s_gld.n += 16
        K.barrier()
        NTile = NC * 4
        t0 = s_t.n; v0 = s_v.n
        with nc.Block() as B:
            @B.tensor
            def _(e):
                s_gld.wait(e)
                for i in range(NTile):
                    oc, ts = divmod(i, 4)
                    if i >= 2:
                        e.wait_ge(s_v.h, v0 + i - 1)
                    for mm_ in range(NM):
                        e.matmul(pw[i % 2],
                                 wo1[:, mm_ * 1024 + oc * 128: mm_ * 1024 + oc * 128 + 128],
                                 outF[:, mm_ * S + ts * 512: mm_ * S + (ts + 1) * 512],
                                 start=(mm_ == 0), stop=(mm_ == NM - 1))
                    e.drain()
                    e.nop().then_inc(s_t.h, 1)
                s_t.n += NTile
            @B.vector
            def _(e):
                for i in range(NTile):
                    oc, ts = divmod(i, 4)
                    e.wait_ge(s_t.h, t0 + i + 1)
                    if i >= 2:
                        e.wait_ge(s_st[i % 2].h, 16 * (i // 2))
                    e.tensor_copy(ob[:, (i % 2) * 512:(i % 2) * 512 + 512], pw[i % 2])
                    e.drain()
                    e.nop().then_inc(s_v.h, 1)
                s_v.n += NTile
            @B.sync
            def _(e):
                for i in range(NTile):
                    oc, ts = divmod(i, 4)
                    e.wait_ge(s_v.h, v0 + i + 1)
                    e.dma_start(out=a2F[oc, :, ts * 512:(ts + 1) * 512],
                                in_=ob[:, (i % 2) * 512:(i % 2) * 512 + 512]
                                ).then_inc(s_st[i % 2].h, 16)
                    s_st[i % 2].n += 16
                s_st[0].wait(e)
                s_st[1].wait(e)
    K.barrier()
    es.close()



def stage_final(K, x1F, a2F, xn2Fb, Wff, bff, logits, u_in, ones128, ident, out_dram):
    nc = K.nc
    Lp = "z_"
    s_gld = K.sem(Lp + "gld")
    s_ld = K.sem(Lp + "ld")
    s_t = K.sem(Lp + "t")
    s_v = K.sem(Lp + "v")
    s_a = K.sem(Lp + "a")
    s_st = [K.sem(Lp + "st0"), K.sem(Lp + "st1")]
    s_wl = [K.sem(Lp + "wl0"), K.sem(Lp + "wl1")]
    es = ExitStack()
    alloc = lambda n, sh, dt: es.enter_context(nc.sbuf_tensor(Lp + n, sh, dt)).ap()
    x1 = alloc("x1", [128, NC * S], F32)
    a2b = alloc("a2b", [128, 8 * 512], F32)
    xnb = alloc("xnb", [128, NC * S], BF16)
    weff = alloc("weff", [128, NC * 1024], BF16)
    wtile = alloc("wtile", [128, 2 * NC * 1024], BF16)
    idt = alloc("idt", [128, 128], F32)
    ones = alloc("ones", [128, 128], F32)
    lu = alloc("lu", [1, 32], F32)
    oh = alloc("oh", [1, 8], F32)
    ohT = alloc("ohT", [8, 1], F32)
    ohb = alloc("ohb", [128, 8], F32)
    brow = alloc("brow", [8, 1024], F32)
    brow2 = alloc("brow2", [1, 1024], F32)
    beff = alloc("beff", [128, 8], F32)
    obuf = alloc("obuf", [128, 2 * 1024], F32)
    with (
        nc.psum_tensor(Lp + "pf0", [128, 512], F32) as pf0_,
        nc.psum_tensor(Lp + "pf1", [128, 512], F32) as pf1_,
        nc.psum_tensor(Lp + "pb", [128, 1024], F32) as pb_,
        nc.psum_tensor(Lp + "ptx", [128, 2 * 128], F32) as ptx_,
        nc.psum_tensor(Lp + "ptxA", [128, 128], F32) as ptxA_,
        nc.psum_tensor(Lp + "ptxB", [128, 128], F32) as ptxB_,
    ):
        pf = [pf0_.ap(), pf1_.ap()]
        pb = pb_.ap()
        ptx = ptx_.ap()
        ptxp = [ptxA_.ap(), ptxB_.ap()]
        with nc.Block() as B:
            @B.sync
            def _(e):
                e.dma_start(out=idt, in_=ident).then_inc(s_ld.h, 16); s_ld.n += 16
                e.dma_start(out=ones, in_=ones128).then_inc(s_ld.h, 16); s_ld.n += 16
                e.dma_start(out=lu[:, 0:8], in_=logits.rearrange("(a n) -> a n", a=1)
                            ).then_inc(s_ld.h, 16); s_ld.n += 16
                e.dma_start(out=lu[:, 8:16], in_=u_in.rearrange("(a n) -> a n", a=1)
                            ).then_inc(s_ld.h, 16); s_ld.n += 16
                e.dma_start(out=x1.rearrange("p (c t) -> p c t", c=NC),
                            in_=x1F.rearrange("c p t -> p c t")).then_inc(s_ld.h, 16)
                s_ld.n += 16
                e.dma_start(out=xnb.rearrange("p (c t) -> p c t", c=NC),
                            in_=xn2Fb.rearrange("c p t -> p c t")).then_inc(s_ld.h, 16)
                s_ld.n += 16
                e.dma_start(out=brow, in_=bff).then_inc(s_ld.h, 16); s_ld.n += 16
            @B.scalar
            def _(e):
                s_ld.wait(e)
                e.activation(lu[:, 16:24], lu[:, 8:16], AF.Ln)
                e.drain()
                e.activation(lu[:, 24:32], lu[:, 16:24], AF.Ln, scale=K.cf(-1.0)[0:1, :])
                e.drain()
                e.nop().then_inc(s_a.h, 1); s_a.n += 1
            @B.vector
            def _(e):
                s_a.wait(e)
                e.tensor_sub(lu[:, 16:24], lu[:, 0:8], lu[:, 24:32])
                e.drain()
                e.reduce_max(lu[:, 30:31], lu[:, 16:24], AX.X)
                e.drain()
                e.tensor_scalar(oh, lu[:, 16:24], lu[:, 30:31], None, ALU.is_ge)
                e.drain()
                milestone(e, s_v)
            @B.tensor
            def _(e):
                s_v.wait(e)
                e.matmul(ptx[:, 0:8], ones[0:1, :], oh, start=True, stop=True)
                e.transpose(ptx[0:8, 8:9], oh, idt[0:1, 0:1])
                e.drain()
                e.nop().then_inc(s_t.h, 1); s_t.n += 1
            @B.vector
            def _(e):
                s_t.wait(e)
                e.tensor_copy(ohb, ptx[:, 0:8])
                e.tensor_copy(ohT, ptx[0:8, 8:9])
                milestone(e, s_v)
            @B.tensor
            def _(e):
                s_v.wait(e)
                e.matmul(pb[0:1, 0:512], ohT, brow[:, 0:512], start=True, stop=True)
                e.matmul(pb[0:1, 512:1024], ohT, brow[:, 512:1024], start=True, stop=True)
                e.drain()
                e.nop().then_inc(s_t.h, 1); s_t.n += 1
            @B.vector
            def _(e):
                s_t.wait(e)
                e.tensor_copy(brow2, pb[0:1, :])
                milestone(e, s_v)
            @B.tensor
            def _(e):
                s_v.wait(e)
                for cc in range(NC):
                    e.transpose(ptx[:, 16 + cc:17 + cc], brow2[:, cc * 128:(cc + 1) * 128],
                                idt[0:1, 0:1])
                e.drain()
                e.nop().then_inc(s_t.h, 1); s_t.n += 1
            @B.vector
            def _(e):
                s_t.wait(e)
                e.tensor_copy(beff, ptx[:, 16:24])
                milestone(e, s_v)
        K.barrier()
        wl0 = [s_wl[0].n, s_wl[1].n]
        v0 = s_v.n
        with nc.Block() as B:
            @B.gpsimd
            def _(e):
                for i in range(8):
                    if i >= 2:
                        e.wait_ge(s_v.h, v0 + i - 1)
                    e.dma_start(out=wtile[:, (i % 2) * NC * 1024:((i % 2) + 1) * NC * 1024
                                          ].rearrange("p (c n) -> p c n", c=NC),
                                in_=Wff[i].rearrange("(c p) n -> p c n", p=128)
                                ).then_inc(s_wl[i % 2].h, 16)
                    s_wl[i % 2].n += 16
            @B.vector
            def _(e):
                for i in range(8):
                    e.wait_ge(s_wl[i % 2].h, wl0[i % 2] + 16 * (i // 2 + 1))
                    wt = wtile[:, (i % 2) * NC * 1024:((i % 2) + 1) * NC * 1024]
                    if i == 0:
                        e.tensor_scalar_mul(weff, wt, ohb[:, 0:1])
                    else:
                        e.scalar_tensor_tensor(weff, wt, ohb[:, i:i + 1], weff,
                                               ALU.mult, ALU.add)
                    e.drain()
                    e.nop().then_inc(s_v.h, 1)
                s_v.n += 8
        K.barrier()
        NTile = NC * 4
        s_al = K.sem(Lp + "al")
        for ts in range(4):
            t0 = s_t.n; v0 = s_v.n; al0 = s_al.n
            with nc.Block() as B:
                @B.sync
                def _(e):
                    for oc in range(NC):
                        e.dma_start(out=a2b[:, oc * 512:(oc + 1) * 512],
                                    in_=a2F[oc, :, ts * 512:(ts + 1) * 512]).then_inc(s_al.h, 16)
                        s_al.n += 16
                @B.tensor
                def _(e):
                    for oc in range(NC):
                        if oc >= 2:
                            e.wait_ge(s_v.h, v0 + 2 * (oc - 1))
                        for dd in range(NC):
                            e.matmul(pf[oc % 2],
                                     weff[:, dd * 1024 + oc * 128: dd * 1024 + oc * 128 + 128],
                                     xnb[:, dd * S + ts * 512: dd * S + (ts + 1) * 512],
                                     start=(dd == 0), stop=(dd == NC - 1))
                        e.drain()
                        e.nop().then_inc(s_t.h, 1)
                    s_t.n += NC
                @B.vector
                def _(e):
                    e.wait_ge(s_al.h, al0 + 16 * NC)
                    for oc in range(NC):
                        e.wait_ge(s_t.h, t0 + oc + 1)
                        xs = x1[:, oc * S + ts * 512: oc * S + (ts + 1) * 512]
                        e.scalar_tensor_tensor(xs, pf[oc % 2], beff[:, oc:oc + 1], xs,
                                               ALU.add, ALU.add)
                        e.drain()
                        e.tensor_add(xs, xs, a2b[:, oc * 512:(oc + 1) * 512])
                        e.drain()
                        e.nop().then_inc(s_v.h, 2)
                    s_v.n += 2 * NC
            K.barrier()
        t0 = s_t.n; v0 = s_v.n
        st0 = [0, 0]
        with nc.Block() as B:
            @B.tensor
            def _(e):
                for tt in range(NT):
                    for cc in range(NC):
                        i = tt * NC + cc
                        if i >= 2:
                            e.wait_ge(s_v.h, v0 + i - 1)
                        e.transpose(ptxp[i % 2],
                                    x1[:, cc * S + tt * 128: cc * S + tt * 128 + 128],
                                    idt)
                        e.drain()
                        e.nop().then_inc(s_t.h, 1)
                s_t.n += NT * NC
            @B.vector
            def _(e):
                for tt in range(NT):
                    if tt >= 2:
                        e.wait_ge(s_st[tt % 2].h, 16 * (tt // 2))
                    for cc in range(NC):
                        i = tt * NC + cc
                        e.wait_ge(s_t.h, t0 + i + 1)
                        e.tensor_copy(obuf[:, (tt % 2) * 1024 + cc * 128:
                                           (tt % 2) * 1024 + (cc + 1) * 128],
                                      ptxp[i % 2])
                        e.drain()
                        e.nop().then_inc(s_v.h, 1)
                s_v.n += NT * NC
            @B.sync
            def _(e):
                for tt in range(NT):
                    e.wait_ge(s_v.h, v0 + (tt + 1) * NC)
                    e.dma_start(out=out_dram[tt * 128:(tt + 1) * 128, :],
                                in_=obuf[:, (tt % 2) * 1024:(tt % 2) * 1024 + 1024]
                                ).then_inc(s_st[tt % 2].h, 16)
                    s_st[tt % 2].n += 16
                s_st[0].wait(e)
                s_st[1].wait(e)
    K.barrier()
    es.close()



def _mk_io(nc, specs):
    out = {}
    for name, shape, dt, kind in specs:
        out[name] = nc.dram_tensor(name, list(shape), dt, kind=kind).ap()
    return out


def kernel(**inputs):
    C = host_constants()
    NCORE = 8
    EI, EO = "ExternalInput", "ExternalOutput"
    FS = [NC, 128, S]

    def ttile(tab):
        return np.ascontiguousarray(tab.reshape(NT, 128, 64).transpose(1, 0, 2).reshape(128, NT * 64))

    base_consts = {"constv": C["constv"], "ident": C["ident"], "ones128": C["ones128"]}

    nc = bass.Bass("TRN2", target_bir_lowering=False, debug=False)
    io = _mk_io(nc, [("constv", (len(CONSTS),), F32, EI), ("ident", (128, 128), F32, EI),
                     ("x_in", (S, D), F32, EI), ("xF", FS, F32, EO), ("xnF", FS, F32, EO)])
    K = Kd(nc); K.init_consts(io["constv"])
    stage_prep(K, io["x_in"], io["ident"], io["xF"], io["xnF"])
    maps = [{"constv": C["constv"], "ident": C["ident"],
             "x_in": np.ascontiguousarray(inputs["L0_attn_x"][b])} for b in range(NCORE)]
    res = run_bass_kernel_spmd(nc, maps, core_ids=list(range(NCORE))).results
    xF = [r["xF"] for r in res]; xnF = [r["xnF"] for r in res]

    nc = bass.Bass("TRN2", target_bir_lowering=False, debug=False)
    io = _mk_io(nc, [("constv", (len(CONSTS),), F32, EI), ("ident", (128, 128), F32, EI),
                     ("triT128", (128, 128), F32, EI),
                     ("xnF", FS, F32, EI), ("xF", FS, F32, EI),
                     ("Wq", (D, D), F32, EI), ("bq", (D,), F32, EI),
                     ("Wk", (D, D), F32, EI), ("bk", (D,), F32, EI),
                     ("Wv", (D, D), F32, EI), ("bv", (D,), F32, EI),
                     ("Wo", (D, D), F32, EI), ("bo", (D,), F32, EI),
                     ("accF", FS, F32, EO)])
    K = Kd(nc); K.init_consts(io["constv"])
    stage_attn0(K, io["xnF"], io["xF"], io["Wq"], io["bq"], io["Wk"], io["bk"],
                io["Wv"], io["bv"], io["Wo"], io["bo"], io["ident"], io["triT128"],
                io["accF"])
    maps = [{"constv": C["constv"], "ident": C["ident"], "triT128": C["triT128"],
             "xnF": xnF[b], "xF": xF[b],
             "Wq": inputs["L0_attn_Wq"], "bq": inputs["L0_attn_bq"],
             "Wk": inputs["L0_attn_Wk"], "bk": inputs["L0_attn_bk"],
             "Wv": inputs["L0_attn_Wv"], "bv": inputs["L0_attn_bv"],
             "Wo": inputs["L0_attn_Wo"], "bo": inputs["L0_attn_bo"]} for b in range(NCORE)]
    res = run_bass_kernel_spmd(nc, maps, core_ids=list(range(NCORE))).results
    accF = [r["accF"] for r in res]

    yF = xnF
    for li in range(6):
        nc = bass.Bass("TRN2", target_bir_lowering=False, debug=False)
        io = _mk_io(nc, [("constv", (len(CONSTS),), F32, EI), ("ident", (128, 128), F32, EI),
                         ("ones128", (128, 128), F32, EI),
                         ("yFi", FS, F32, EI), ("yFo", FS, F32, EO),
                         ("Wqkv", (6, 1024, 3072), F32, EI), ("q_scale", (6, 64), F32, EI),
                         ("k_scale", (6, 64), F32, EI), ("Wo", (6, 1024, 1024), F32, EI),
                         ("ln_g", (6, 1024), F32, EI), ("ln_b", (6, 1024), F32, EI),
                         ("W1", (6, 1024, 5460), F32, EI), ("b1", (6, 5460), F32, EI),
                         ("W2", (6, 2730, 1024), F32, EI), ("b2", (6, 1024), F32, EI),
                         ("csq", (128, NT * 64), F32, EI), ("ssq", (128, NT * 64), F32, EI),
                         ("csk", (128, NT * 64), F32, EI), ("ssk", (128, NT * 64), F32, EI),
                         ("mask_gen", (128, 384), F32, EI), ("mask_t0", (128, 128), F32, EI),
                         ("mask_t1", (128, 256), F32, EI)])
        K = Kd(nc); K.init_consts(io["constv"])
        stage_ffn_layer(K, li, io["yFi"], io["yFo"], io["Wqkv"], io["q_scale"],
                        io["k_scale"], io["Wo"], io["ln_g"], io["ln_b"], io["W1"],
                        io["b1"], io["W2"], io["b2"], io["ident"], io["ones128"],
                        io["csq"], io["ssq"], io["csk"], io["ssk"],
                        io["mask_gen"], io["mask_t0"], io["mask_t1"])
        wmap = {"constv": C["constv"], "ident": C["ident"], "ones128": C["ones128"],
                "csq": ttile(C["rot_csq"]), "ssq": ttile(C["rot_ssq"]),
                "csk": ttile(C["rot_csk"]), "ssk": ttile(C["rot_ssk"]),
                "mask_gen": C["mask_gen"], "mask_t0": C["mask_t0"], "mask_t1": C["mask_t1"]}
        for k in ("Wqkv", "q_scale", "k_scale", "Wo", "ln_g", "ln_b", "W1", "b1", "W2", "b2"):
            wmap[k] = inputs["L0_ffn_" + k]
        maps = [dict(wmap, yFi=yF[b]) for b in range(NCORE)]
        res = run_bass_kernel_spmd(nc, maps, core_ids=list(range(NCORE))).results
        yF = [r["yFo"] for r in res]

    nc = bass.Bass("TRN2", target_bir_lowering=False, debug=False)
    io = _mk_io(nc, [("constv", (len(CONSTS),), F32, EI), ("ones128", (128, 128), F32, EI),
                     ("accF", FS, F32, EI), ("yF", FS, F32, EI),
                     ("x1F", FS, F32, EO), ("xn2Fb", FS, BF16, EO)])
    K = Kd(nc); K.init_consts(io["constv"])
    stage_mid(K, io["accF"], io["yF"], io["ones128"], io["x1F"], io["xn2Fb"])
    maps = [{"constv": C["constv"], "ones128": C["ones128"], "accF": accF[b], "yF": yF[b]}
            for b in range(NCORE)]
    res = run_bass_kernel_spmd(nc, maps, core_ids=list(range(NCORE))).results
    x1F = [r["x1F"] for r in res]; xn2Fb = [r["xn2Fb"] for r in res]

    nc = bass.Bass("TRN2", target_bir_lowering=False, debug=False)
    io = _mk_io(nc, [("constv", (len(CONSTS),), F32, EI), ("ident", (128, 128), F32, EI),
                     ("ones128", (128, 128), F32, EI), ("tri16x8", (16, 128), F32, EI),
                     ("bsel128", (96,), F32, EI), ("xn2Fb", FS, BF16, EI),
                     ("W_sel", (1024, 160), F32, EI), ("dt_W", (32, 512), F32, EI),
                     ("dt_b", (512,), F32, EI), ("Wk", (1024, 512), F32, EI),
                     ("Wv", (1024, 512), F32, EI), ("Wr", (1024, 512), F32, EI),
                     ("Wg", (1024, 512), F32, EI), ("Wo", (512, 1024), F32, EI),
                     ("base_decay", (8, 64), F32, EI), ("time_first", (8, 64), F32, EI),
                     ("a2F", FS, F32, EO)])
    K = Kd(nc); K.init_consts(io["constv"])
    stage_l1attn(K, io["xn2Fb"], io["W_sel"], io["dt_W"], io["dt_b"], io["Wk"], io["Wv"],
                 io["Wr"], io["Wg"], io["Wo"], io["base_decay"], io["time_first"],
                 io["ones128"], io["ident"], io["tri16x8"], io["bsel128"], io["a2F"])
    maps = [{"constv": C["constv"], "ident": C["ident"], "ones128": C["ones128"],
             "tri16x8": C["triT16x8"], "bsel128": C["bsel128"], "xn2Fb": xn2Fb[b],
             "W_sel": inputs["L1_attn_W_sel"], "dt_W": inputs["L1_attn_dt_W"],
             "dt_b": inputs["L1_attn_dt_b"], "Wk": inputs["L1_attn_Wk"],
             "Wv": inputs["L1_attn_Wv"], "Wr": inputs["L1_attn_Wr"],
             "Wg": inputs["L1_attn_Wg"], "Wo": inputs["L1_attn_Wo"],
             "base_decay": inputs["L1_attn_base_decay"],
             "time_first": inputs["L1_attn_time_first"]} for b in range(NCORE)]
    res = run_bass_kernel_spmd(nc, maps, core_ids=list(range(NCORE))).results
    a2F = [r["a2F"] for r in res]

    nc = bass.Bass("TRN2", target_bir_lowering=False, debug=False)
    io = _mk_io(nc, [("constv", (len(CONSTS),), F32, EI), ("ident", (128, 128), F32, EI),
                     ("ones128", (128, 128), F32, EI),
                     ("x1F", FS, F32, EI), ("a2F", FS, F32, EI), ("xn2Fb", FS, BF16, EI),
                     ("Wff", (8, 1024, 1024), F32, EI), ("bff", (8, 1024), F32, EI),
                     ("logits", (8,), F32, EI), ("u_in", (8,), F32, EI),
                     ("outd", (S, D), F32, EO)])
    K = Kd(nc); K.init_consts(io["constv"])
    stage_final(K, io["x1F"], io["a2F"], io["xn2Fb"], io["Wff"], io["bff"],
                io["logits"], io["u_in"], io["ones128"], io["ident"], io["outd"])
    maps = [{"constv": C["constv"], "ident": C["ident"], "ones128": C["ones128"],
             "x1F": x1F[b], "a2F": a2F[b], "xn2Fb": xn2Fb[b],
             "Wff": inputs["L1_ffn_W"], "bff": inputs["L1_ffn_b"],
             "logits": inputs["L1_ffn_logits"], "u_in": inputs["L1_ffn_u"]}
            for b in range(NCORE)]
    res = run_bass_kernel_spmd(nc, maps, core_ids=list(range(NCORE))).results
    return np.stack([np.asarray(r["outd"]) for r in res], axis=0)
